# version 49
# baseline (speedup 1.0000x reference)
import numpy as np
import contextlib
import concourse.bass as bass
import concourse.mybir as mybir
from concourse.bass_utils import run_bass_kernel_spmd
from concourse.alu_op_type import AluOpType

F32 = mybir.dt.float32
BF16 = mybir.dt.bfloat16
I32 = mybir.dt.int32
AF = mybir.ActivationFunctionType
OP = AluOpType

NCORES = 8
D = 1024
DT = 8
DH = 64
H = 3072
HT = 24
NTOK = 4096
TC = 256
NCH = NTOK // TC
KAPPA = float(1.0 / 127.0 ** 2)
INVKAPPA = float(127.0 ** 2)
INV127 = float(1.0 / 127.0)
INV_SQRT2 = 0.7071067811865476
EPS = 1e-5
INVD = float(1.0 / D)

_FFN_SEQ = []
for _j in range(HT):
    _FFN_SEQ.append(("G", _j))
    if _j >= 1:
        _FFN_SEQ.append(("D", _j - 1))
_FFN_SEQ.append(("D", HT - 1))

PE_SLOTS = {
    0: ["ST2a"], 1: ["ST2b"], 2: ["BC2"],
    5: ["H"], 7: ["R0"], 8: ["R1"], 9: ["R2"], 10: ["R3"],
    12: ["ST1a"], 13: ["ST1b"], 14: ["BC1"],
}
DVE_SLOTS = {
    0: ["Q2_45"], 1: ["Q2_67"], 2: ["FIN2"], 3: ["BCT2"],
    8: ["SE0"], 9: ["SE1", "STQ01"], 10: ["SE2", "STQ23"],
    11: ["SE3", "STQ45"], 12: ["STQ67", "FIN1"],
    14: ["BCT1"], 16: ["Y1_01"], 17: ["Y1_23"], 18: ["Y1_45", "Y1_67"],
}
GP_SEQ = ["SUB01", "SUB23", "SUB45", "SUB67",
          "MUL01", "MUL23", "MUL45", "MUL67", "XL"]
SC_SLOTS = {
    5: ["GELU"],
    7: ["RS2"],
    15: ["RS1"],
}


class WaitTracker:
    def __init__(self):
        self.seen = {}

    def need(self, sem, val):
        if val is None or val <= 0:
            return False
        if self.seen.get(sem.name, -1) >= val:
            return False
        self.seen[sem.name] = val
        return True


class Plan:
    def __init__(self):
        self.ev = {}
        self.cnt = {}
        self.recording = True
        self.trackers = {}
        self.prog = {}
        self.cur = None
        self.chain_vals = []
        self.chain_idx = 0

    def begin_stream(self, name):
        self.cur = self.prog.setdefault(name, [])

    def inc(self, ins, sem, key=None, amount=1):
        if self.recording:
            c = self.cnt.get(sem.name, 0) + amount
            self.cnt[sem.name] = c
            if key is not None:
                assert key not in self.ev, key
                self.ev[key] = c
            self.cur.append(("inc", sem.name, amount, key))
        else:
            ins.then_inc(sem, amount)

    def chain(self, ins, eng, sem):
        if self.recording:
            c = self.cnt.get(sem.name, 0) + 1
            self.cnt[sem.name] = c
            self.chain_vals.append(c)
            self.cur.append(("inc", sem.name, 1, None))
        else:
            ins.then_inc(sem, 1)
            eng.wait_ge(sem, self.chain_vals[self.chain_idx])
            self.chain_idx += 1

    def mark(self, eng, sem):
        if self.recording:
            self.chain_vals.append(self.cnt.get(sem.name, 0))
        else:
            val = self.chain_vals[self.chain_idx]
            self.chain_idx += 1
            if val > 0:
                eng.wait_ge(sem, val)

    def wait(self, eng, sem, key):
        if self.recording:
            if isinstance(key, int):
                if key > 0:
                    self.cur.append(("wait", sem.name, key, None))
            else:
                self.cur.append(("wait", sem.name, None, key))
            return
        if isinstance(key, int):
            val = key
        else:
            val = self.ev.get(key)
            if val is None:
                assert not (0 <= key[1] <= NCH - 1), f"missing event {key}"
                return
        tr = self.trackers.setdefault(id(eng), WaitTracker())
        if tr.need(sem, val):
            eng.wait_ge(sem, val)

    def verify_no_deadlock(self):
        pos = {k: 0 for k in self.prog}
        sems = {}
        progress = True
        while progress:
            progress = False
            for name, ops in self.prog.items():
                i = pos[name]
                while i < len(ops):
                    op, semn, val, key = ops[i]
                    if op == "wait":
                        if key is not None:
                            val = self.ev.get(key)
                        if val is not None and sems.get(semn, 0) < val:
                            break
                    else:
                        sems[semn] = sems.get(semn, 0) + val
                        progress = progress or True
                    i += 1
                    progress = True
                pos[name] = i
        stuck = {k: (pos[k], len(v)) for k, v in self.prog.items()
                 if pos[k] < len(v)}
        if stuck:
            detail = {}
            for k in stuck:
                op = self.prog[k][pos[k]]
                detail[k] = op
            raise AssertionError(f"plan deadlock: {stuck} blocked at {detail} "
                                 f"sems={sems}")


class FakeIns:
    def then_inc(self, *a, **k):
        return self


class FakeEng:
    def __getattr__(self, name):
        return lambda *a, **k: FakeIns()

    def wait_ge(self, *a, **k):
        pass


def build_nc():
    nc = bass.Bass("TRN2", target_bir_lowering=False)

    xT = nc.dram_tensor("xT", [D, NTOK], F32, kind="ExternalInput")
    wkN = nc.dram_tensor("wkN", [D, D], F32, kind="ExternalInput")
    w1T = nc.dram_tensor("w1T", [D, DH], F32, kind="ExternalInput")
    w2T = nc.dram_tensor("w2T", [DH, D], F32, kind="ExternalInput")
    wgT = nc.dram_tensor("wgT", [D, H], I32, kind="ExternalInput")
    wuT = nc.dram_tensor("wuT", [D, H], I32, kind="ExternalInput")
    wdT = nc.dram_tensor("wdT", [H, D], I32, kind="ExternalInput")
    idk = nc.dram_tensor("idk", [128, 128], F32, kind="ExternalInput")
    outT = nc.dram_tensor("outT", [D, NTOK], F32, kind="ExternalOutput")

    xTr = xT.rearrange("(a p) t -> p a t", p=128)
    outTr = outT.rearrange("(a p) t -> p a t", p=128)

    es = contextlib.ExitStack()
    def sb(name, shape, dtype):
        return es.enter_context(nc.sbuf_tensor(name, shape, dtype))
    def psum(name):
        return es.enter_context(nc.psum_tensor(name, [128, 512], F32))
    def sem(name):
        return es.enter_context(nc.semaphore(name))

    with es:
        WK = sb("WK", [128, DT, D], BF16)
        W1 = sb("W1", [128, DT, DH], BF16)
        W2 = sb("W2", [DH, D], BF16)
        WG = sb("WG", [128, DT, H], BF16)
        WU = sb("WU", [128, DT, H], BF16)
        WD = sb("WD", [128, HT, D], BF16)
        IDK = sb("IDK", [128, 128], BF16)
        XF = sb("XF", [128, 3, DT, TC], BF16)
        OUTB = sb("OUTB", [128, 4, TC], F32)
        Y1 = sb("Y1", [128, 2, DT, TC], BF16)
        SQT = sb("SQT", [128, 4, TC], BF16)
        SG = sb("SG", [128, 4, TC], BF16)
        H2 = sb("H2", [128, 4, TC], BF16)
        AG = sb("AG", [DH, TC], BF16)
        ONES = sb("ONES", [128, 1], BF16)
        ONESROW = sb("ONESROW", [1, 128], BF16)
        EPST = sb("EPST", [128, 1], F32)
        SROW = sb("SROW", [1, 512], BF16)
        MUT = sb("MUT", [128, TC], F32)
        VART = sb("VART", [128, TC], F32)
        MSQT = sb("MSQT", [128, TC], F32)
        RT = sb("RT", [128, TC], F32)
        MUT2 = sb("MUT2", [128, TC], F32)
        VART2 = sb("VART2", [128, TC], F32)
        RT2 = sb("RT2", [128, TC], F32)
        MSQT2 = sb("MSQT2", [128, TC], F32)
        SUBT = sb("SUBT", [128, DT, TC], BF16)

        P = [psum(f"P{i}") for i in range(8)]
        PDB = [P[0], P[1], P[2], P[7]]
        PGU = [P[3], P[4]]
        def KB(g): return P[5] if g % 2 == 0 else P[6]
        def RB(g): return P[6] if g % 2 == 0 else P[5]
        PH = P[5]
        PST1, PBC1 = P[6], P[5]
        PST2, PBC2 = P[5], P[6]

        s_t = sem("s_t")
        s_v = sem("s_v")
        s_s = sem("s_s")
        s_x3 = [sem(f"s_x{k}") for k in range(3)]
        s_ob = [sem(f"s_ob{k}") for k in range(4)]
        s_wk = sem("s_wk")
        s_w1 = sem("s_w1")
        s_w2 = sem("s_w2")
        s_id = sem("s_id")
        s_g = sem("s_g")
        s_wg = [sem(f"s_wg{q}") for q in range(8)]
        s_wu = [sem(f"s_wu{q}") for q in range(8)]
        s_wd = [sem(f"s_wd{q}") for q in range(8)]

        E_ = Plan()

        def attn_item(t, item, c):
            if item == "H":
                E_.wait(t, s_x3[c % 3], 16 * (c // 3 + 1))
                E_.wait(t, s_v, ("v", -1, "W1K"))
                E_.wait(t, s_v, ("v", c - 2, "FIN2"))
                for a in range(DT):
                    ins = t.matmul(PH[0:DH, 0:TC], W1[:, a, :],
                                   XF[:, c % 3, a, :],
                                   start=(a == 0), stop=(a == DT - 1))
                E_.inc(ins, s_t, ("t", c, "H"))
            elif item.startswith("R"):
                g = int(item[1])
                bank = RB(g)
                if c == 0:
                    E_.wait(t, s_w2, 16)
                E_.wait(t, s_s, ("s", c, "AG"))
                if g == 0:
                    E_.wait(t, s_v, ("v", c - 2, "BCT2"))
                elif g == 1:
                    pass
                else:
                    E_.wait(t, s_v, ("v", c, f"SE{g - 2}"))
                for oh in range(2):
                    o = 2 * g + oh
                    ins = t.matmul(bank[:, oh * TC:(oh + 1) * TC],
                                   W2[0:DH, o * 128:(o + 1) * 128],
                                   AG[0:DH, :], start=(oh == 0), stop=(oh == 1))
                E_.inc(ins, s_t, ("t", c, f"R{g}"))
            elif item in ("ST1a", "ST1b"):
                lo = 0 if item == "ST1a" else 4
                if lo == 0:
                    E_.wait(t, s_v, ("v", c, "SE2"))
                for a in range(lo, lo + 4):
                    E_.wait(t, s_v, ("v", c, f"SE{a // 2}"))
                    t.matmul(PST1[0:1, 0:TC], ONES[:, 0:1],
                             XF[:, c % 3, a, :],
                             start=(a == 0), stop=False)
                    E_.wait(t, s_v, ("v", c, f"STQ{a}"))
                    E_.inc(t.matmul(PST1[0:1, TC:2 * TC], ONES[:, 0:1],
                                    SQT[:, a % 4, :],
                                    start=False, stop=(a == DT - 1)),
                           s_t, ("t", c, f"ST1_{a}"))
            elif item == "BC1":
                E_.wait(t, s_v, ("v", c, "SE3"))
                E_.wait(t, s_v, ("v", c, "FIN1"))
                E_.inc(t.matmul(PBC1[:, 0:512], ONESROW[0:1, 0:128],
                                SROW[0:1, 0:512], start=True, stop=True),
                       s_t, ("t", c, "BC1"))
            elif item in ("ST2a", "ST2b"):
                lo = 0 if item == "ST2a" else 4
                if lo == 0:
                    E_.wait(t, s_v, ("v", c + 1, "BCT1"))
                for a in range(lo, lo + 4):
                    E_.wait(t, s_s, ("s", c, f"DE{a // 2}"))
                    t.matmul(PST2[0:1, 0:TC], ONES[:, 0:1],
                             XF[:, c % 3, a, :],
                             start=(a == 0), stop=False)
                    E_.wait(t, s_v, ("v", c, f"SQ2_{a}"))
                    E_.inc(t.matmul(PST2[0:1, TC:2 * TC], ONES[:, 0:1],
                                    SQT[:, a % 4, :],
                                    start=False, stop=(a == DT - 1)),
                           s_t, ("t", c, f"ST2_{a}"))
            elif item == "BC2":
                E_.wait(t, s_v, ("v", c + 1, "FIN1"))
                E_.wait(t, s_v, ("v", c, "FIN2"))
                E_.inc(t.matmul(PBC2[:, 0:512], ONESROW[0:1, 0:128],
                                SROW[0:1, 0:512], start=True, stop=True),
                       s_t, ("t", c, "BC2"))
            else:
                raise AssertionError(item)

        def emit_tensor(t):
            E_.wait(t, s_wk, 16)
            E_.wait(t, s_w1, 16)
            for hb in range(DT):
                for kb in range(DT):
                    ins = t.matmul(P[5][:, hb * DH:(hb + 1) * DH],
                                   WK[:, kb, hb * 128:(hb + 1) * 128],
                                   W1[:, kb, :],
                                   start=(hb == 0 and kb == 0),
                                   stop=(hb == DT - 1 and kb == DT - 1))
            E_.inc(ins, s_t, ("t", -1, "W1K"))
            for item in ["H", "R0", "R1", "R2", "R3", "ST1a", "ST1b", "BC1"]:
                attn_item(t, item, 0)
            for i in range(NCH + 1):
                if i <= NCH - 1:
                    for kind, j in _FFN_SEQ:
                        if kind == "G":
                            bank = PGU[j % 2]
                            if i == 0:
                                E_.wait(t, s_wg[j // 3], 16)
                                E_.wait(t, s_wu[j // 3], 16)
                            if j >= 2:
                                E_.wait(t, s_v, ("v", i, f"H2_{j - 2}"))
                            else:
                                E_.wait(t, s_v, ("v", i, "Y1_3"))
                                E_.wait(t, s_v, ("v", i, "Y1_7"))
                                E_.wait(t, s_v, ("v", i - 1, f"H2_{22 + j}"))
                            for a in range(DT):
                                ins = t.matmul(bank[:, 0:TC],
                                               WG[:, a, j * 128:(j + 1) * 128],
                                               Y1[:, i % 2, a, :], start=(a == 0),
                                               stop=False)
                            for a in range(DT):
                                ins = t.matmul(bank[:, TC:2 * TC],
                                               WU[:, a, j * 128:(j + 1) * 128],
                                               Y1[:, i % 2, a, :], start=False,
                                               stop=(a == DT - 1))
                            E_.inc(ins, s_t, ("t", i, f"TG{j}"))
                        else:
                            if i == 0:
                                E_.wait(t, s_wd[j // 3], 16)
                                if j == 0:
                                    E_.wait(t, s_id, 16)
                            for item in PE_SLOTS.get(j, []):
                                if item in ("ST2a", "ST2b", "BC2"):
                                    if i >= 1:
                                        attn_item(t, item, i - 1)
                                elif i + 1 <= NCH - 1:
                                    attn_item(t, item, i + 1)
                            E_.wait(t, s_v, ("v", i, f"H2_{j}"))
                            if j == 0:
                                E_.wait(t, s_s, ("s", i - 1, "DE3"))
                                for o in range(DT):
                                    t.matmul(PDB[o // 2][:, (o % 2) * TC:(o % 2 + 1) * TC],
                                             IDK[:, :], Y1[:, i % 2, o, :],
                                             start=(o % 2 == 0), stop=False)
                            for o in range(DT):
                                ins = t.matmul(PDB[o // 2][:, (o % 2) * TC:(o % 2 + 1) * TC],
                                               WD[:, j, o * 128:(o + 1) * 128],
                                               H2[:, j % 4, :],
                                               start=False,
                                               stop=(j == HT - 1 and o % 2 == 1))
                            E_.inc(ins, s_t, ("t", i, f"TD{j}"))
                else:
                    for item in ("ST2a", "ST2b", "BC2"):
                        attn_item(t, item, i - 1)

        def dve_item(v, item, i):
            if item.startswith("SE"):
                c, g = i + 1, int(item[2])
                E_.wait(v, s_x3[c % 3], 16 * (c // 3 + 1))
                E_.wait(v, s_t, ("t", c, f"R{g}"))
                E_.inc(v.tensor_tensor(XF[:, c % 3, 2 * g:2 * g + 2, :],
                                       XF[:, c % 3, 2 * g:2 * g + 2, :],
                                       RB(g)[:, 0:512], OP.add),
                       s_v, ("v", c, f"SE{g}"))
            elif item == "FIN1":
                c = i + 1
                E_.wait(v, s_t, ("t", c, "ST1_7"))
                E_.wait(v, s_t, ("t", c - 2, "BC2"))
                E_.inc(v.tensor_copy(SROW[0:1, 0:512], PST1[0:1, 0:512]),
                       s_v, ("v", c, "FIN1"))
            elif item == "BCT1":
                c = i + 1
                E_.wait(v, s_t, ("t", c, "BC1"))
                E_.wait(v, s_v, ("v", c - 1, "Y1_7"))
                E_.chain(v.tensor_scalar(MUT[:], PBC1[:, 0:TC], INVD, None, OP.mult),
                         v, s_v)
                E_.chain(v.scalar_tensor_tensor(MSQT[:], MUT[:], -1.0, MUT[:],
                                                OP.mult, OP.mult), v, s_v)
                E_.inc(v.scalar_tensor_tensor(VART[:], PBC1[:, TC:2 * TC], INVD,
                                              MSQT[:], OP.mult, OP.add),
                       s_v, ("v", c, "BCT1"))
            elif item.startswith("Y1_"):
                c = i + 1
                E_.wait(v, s_v, ("v", c, "SE3"))
                E_.wait(v, s_s, ("s", c, "RS1"))
                E_.wait(v, s_t, ("t", c - 2, "TG23"))
                for a in (int(item[3]), int(item[4])):
                    E_.chain(v.tensor_tensor(XF[:, c % 3, a, :], XF[:, c % 3, a, :],
                                             MUT[:], OP.subtract), v, s_v)
                    E_.inc(v.tensor_tensor(Y1[:, c % 2, a, :], XF[:, c % 3, a, :],
                                           RT[:], OP.mult),
                           s_v, ("v", c, f"Y1_{a}"))
            elif item == "FIN2":
                c = i - 1
                E_.wait(v, s_t, ("t", c, "ST2_7"))
                E_.wait(v, s_t, ("t", c + 1, "BC1"))
                E_.inc(v.tensor_copy(SROW[0:1, 0:512], PST2[0:1, 0:512]),
                       s_v, ("v", c, "FIN2"))
            elif item == "BCT2":
                c = i - 1
                E_.wait(v, s_t, ("t", c, "BC2"))
                E_.wait(v, s_g, ("g", c - 1, "SUB7"))
                E_.chain(v.tensor_scalar(MUT2[:], PBC2[:, 0:TC], INVD, None, OP.mult),
                         v, s_v)
                E_.chain(v.scalar_tensor_tensor(MSQT[:], MUT2[:], -1.0, MUT2[:],
                                                OP.mult, OP.mult), v, s_v)
                E_.inc(v.scalar_tensor_tensor(VART2[:], PBC2[:, TC:2 * TC], INVD,
                                              MSQT[:], OP.mult, OP.add),
                       s_v, ("v", c, "BCT2"))
            elif item.startswith("STQ"):
                c = i + 1
                for a in (int(item[3]), int(item[4])):
                    E_.wait(v, s_v, ("v", c, f"SE{a // 2}"))
                    if a < 4:
                        E_.wait(v, s_t, ("t", c - 2, f"ST2_{4 + a}"))
                    else:
                        E_.wait(v, s_t, ("t", c, f"ST1_{a - 4}"))
                    E_.inc(v.tensor_tensor(SQT[:, a % 4, :],
                                           XF[:, c % 3, a, :],
                                           XF[:, c % 3, a, :], OP.mult),
                           s_v, ("v", c, f"STQ{a}"))
            elif item.startswith("Q2_"):
                c = i - 1
                for a in (int(item[3]), int(item[4])):
                    E_.wait(v, s_s, ("s", c, f"DE{a // 2}"))
                    if a < 4:
                        E_.wait(v, s_t, ("t", c + 1, f"ST1_{4 + a}"))
                    else:
                        E_.wait(v, s_t, ("t", c, f"ST2_{a - 4}"))
                    E_.inc(v.tensor_tensor(SQT[:, a % 4, :],
                                           XF[:, c % 3, a, :],
                                           XF[:, c % 3, a, :], OP.mult),
                           s_v, ("v", c, f"SQ2_{a}"))
            else:
                raise AssertionError(item)

        def emit_vector(v):
            v.memset(ONES[:], 1.0)
            v.memset(EPST[:], EPS)
            E_.inc(v.memset(ONESROW[:], 1.0), s_v, ("v", -1, "INIT"))
            E_.wait(v, s_t, ("t", -1, "W1K"))
            E_.inc(v.tensor_copy(W1[:, :, :], P[5][:, 0:512]),
                   s_v, ("v", -1, "W1K"))
            for item in ["SE0", "STQ01", "SE1", "STQ23", "SE2", "SE3",
                         "STQ45", "STQ67", "FIN1", "BCT1", "Y1_01", "Y1_23",
                         "Y1_45", "Y1_67"]:
                dve_item(v, item, -1)
            for i in range(NCH + 1):
                if i <= NCH - 1:
                    for j in range(HT):
                        E_.wait(v, s_s, ("s", i, f"SIG{j}"))
                        if j >= 4:
                            E_.wait(v, s_t, ("t", i, f"TD{j - 4}"))
                        else:
                            E_.wait(v, s_t, ("t", i - 1, f"TD{20 + j}"))
                        bank = PGU[j % 2]
                        E_.inc(v.tensor_tensor(H2[:, j % 4, :], SG[:, j % 4, :],
                                               bank[:, TC:2 * TC], OP.mult),
                               s_v, ("v", i, f"H2_{j}"))
                        for item in DVE_SLOTS.get(j, []):
                            if item in ("FIN2", "BCT2") or item.startswith("Q2_"):
                                if i >= 1:
                                    dve_item(v, item, i)
                            elif i + 1 <= NCH - 1:
                                dve_item(v, item, i)
                    dve_item(v, "Q2_01", i + 1)
                    dve_item(v, "Q2_23", i + 1)
                else:
                    for item in ["Q2_45", "Q2_67", "FIN2", "BCT2"]:
                        dve_item(v, item, i)

        def sc_item(sc, item, i):
            if item == "GELU":
                c = i + 1
                E_.wait(sc, s_t, ("t", c, "H"))
                E_.wait(sc, s_t, ("t", c - 1, "R3"))
                E_.inc(sc.activation(AG[0:DH, :], PH[0:DH, 0:TC], AF.Gelu),
                       s_s, ("s", c, "AG"))
            elif item == "RS1":
                c = i + 1
                E_.wait(sc, s_v, ("v", c, "BCT1"))
                E_.wait(sc, s_v, ("v", c - 1, "Y1_7"))
                E_.chain(sc.activation(MSQT2[:], VART[:], AF.Ln, bias=EPST[:, 0:1]),
                         sc, s_s)
                E_.inc(sc.activation(RT[:], MSQT2[:], AF.Exp, scale=-0.5),
                       s_s, ("s", c, "RS1"))
            elif item == "RS2":
                c = i - 1
                E_.wait(sc, s_v, ("v", c, "BCT2"))
                E_.wait(sc, s_g, ("g", c - 1, "OUT7"))
                E_.chain(sc.activation(MSQT2[:], VART2[:], AF.Ln, bias=EPST[:, 0:1]),
                         sc, s_s)
                E_.inc(sc.activation(RT2[:], MSQT2[:], AF.Exp, scale=-0.5),
                       s_s, ("s", c, "RS2"))
            elif item.startswith("DE"):
                c, g = i, int(item[2])
                E_.wait(sc, s_t, ("t", c, "TD23"))
                E_.inc(sc.activation(XF[:, c % 3, 2 * g:2 * g + 2, :],
                                     PDB[g][:, 0:512], AF.Copy, scale=KAPPA),
                       s_s, ("s", c, f"DE{g}"))
            else:
                raise AssertionError(item)

        def emit_scalar(sc):
            for item in ["GELU", "RS1"]:
                sc_item(sc, item, -1)
            for i in range(NCH + 1):
                if i <= NCH - 1:
                    for j in range(HT):
                        E_.wait(sc, s_t, ("t", i, f"TG{j}"))
                        if j >= 4:
                            E_.wait(sc, s_v, ("v", i, f"H2_{j - 4}"))
                        else:
                            E_.wait(sc, s_v, ("v", i - 1, f"H2_{20 + j}"))
                        E_.inc(sc.activation(SG[:, j % 4, :], PGU[j % 2][:, 0:TC],
                                             AF.Silu, scale=INV127),
                               s_s, ("s", i, f"SIG{j}"))
                        for item in SC_SLOTS.get(j, []):
                            if item == "RS2":
                                if i >= 1:
                                    sc_item(sc, item, i)
                            elif i + 1 <= NCH - 1:
                                sc_item(sc, item, i)
                    for g in range(4):
                        sc_item(sc, f"DE{g}", i)
                else:
                    sc_item(sc, "RS2", i)

        def gp_item(g, item, i):
            if item == "XL":
                c = i + 2
                E_.inc(g.dma_start(XF[:, c % 3], xTr[:, :, c * TC:(c + 1) * TC]),
                       s_x3[c % 3], None, 16)
            elif item.startswith("SUB"):
                c = i - 1
                E_.wait(g, s_v, ("v", c, "BCT2"))
                E_.mark(g, s_g)
                for a in (int(item[3]), int(item[4])):
                    E_.inc(g.tensor_tensor(SUBT[:, a, :], XF[:, c % 3, a, :],
                                           MUT2[:], OP.subtract),
                           s_g, ("g", c, f"SUB{a}"))
            elif item.startswith("MUL"):
                c = i - 1
                E_.wait(g, s_s, ("s", c, "RS2"))
                E_.mark(g, s_g)
                for a in (int(item[3]), int(item[4])):
                    n = 8 * c + a
                    if n >= 4:
                        E_.wait(g, s_ob[n % 4], 16 * (n // 4))
                    E_.inc(g.tensor_tensor(OUTB[:, a % 4, :], SUBT[:, a, :],
                                           RT2[:], OP.mult), s_g, ("g", c, f"OUT{a}"))
            else:
                raise AssertionError(item)

        def emit_gpsimd(g):
            wgr = wgT.rearrange("(a p) h -> p a h", p=128)
            wur = wuT.rearrange("(a p) h -> p a h", p=128)
            wdr = wdT.rearrange("(j p) o -> p j o", p=128)
            HQ = H // 8
            JQ = HT // 8
            E_.inc(g.dma_start(WK[:], wkN.rearrange("(kb p) h -> p kb h", p=128)),
                   s_wk, None, 16)
            E_.inc(g.dma_start(W1[:], w1T.rearrange("(a p) m -> p a m", p=128)),
                   s_w1, None, 16)
            E_.inc(g.dma_start(XF[:, 0], xTr[:, :, 0:TC]), s_x3[0], None, 16)
            E_.inc(g.dma_start(W2[:], w2T[:, :]), s_w2, None, 16)
            E_.inc(g.dma_start(IDK[:], idk[:, :]), s_id, None, 16)
            for q in range(8):
                E_.inc(g.dma_start(WG[:, :, q * HQ:(q + 1) * HQ],
                                   wgr[:, :, q * HQ:(q + 1) * HQ]), s_wg[q], None, 16)
                E_.inc(g.dma_start(WU[:, :, q * HQ:(q + 1) * HQ],
                                   wur[:, :, q * HQ:(q + 1) * HQ]), s_wu[q], None, 16)
                E_.inc(g.dma_start(WD[:, q * JQ:(q + 1) * JQ, :],
                                   wdr[:, q * JQ:(q + 1) * JQ, :]), s_wd[q], None, 16)
                if q == 0:
                    E_.inc(g.dma_start(XF[:, 1], xTr[:, :, TC:2 * TC]),
                           s_x3[1], None, 16)


            for i in range(NCH + 1):
                for item in GP_SEQ:
                    if item == "XL":
                        if i + 2 <= NCH - 1:
                            gp_item(g, item, i)
                    elif item.startswith("Y1_"):
                        if i + 1 <= NCH - 1:
                            gp_item(g, item, i)
                    else:
                        if i >= 1:
                            gp_item(g, item, i)

        def emit_sync(s):
            for c in range(NCH):
                for a in range(DT):
                    n = 8 * c + a
                    E_.wait(s, s_g, ("g", c, f"OUT{a}"))
                    E_.inc(s.dma_start(outTr[:, a, c * TC:(c + 1) * TC],
                                       OUTB[:, a % 4, :]), s_ob[n % 4], None, 16)
            if not E_.recording:
                for k in range(4):
                    s.wait_ge(s_ob[k], 16 * (8 * NCH // 4))

        fe = FakeEng()
        E_.begin_stream("g"); emit_gpsimd(fe)
        E_.begin_stream("q"); emit_sync(fe)
        E_.begin_stream("t"); emit_tensor(fe)
        E_.begin_stream("v"); emit_vector(fe)
        E_.begin_stream("s"); emit_scalar(fe)
        E_.verify_no_deadlock()
        E_.recording = False

        block = es.enter_context(nc.Block())

        @block.gpsimd
        def _(g):
            emit_gpsimd(g)

        @block.sync
        def _(s):
            emit_sync(s)

        @block.tensor
        def _(t):
            emit_tensor(t)

        @block.vector
        def _(v):
            emit_vector(v)

        @block.scalar
        def _(sc):
            emit_scalar(sc)

    return nc


_NC_CACHE = {}

def _get_nc():
    if "nc" not in _NC_CACHE:
        _NC_CACHE["nc"] = build_nc()
    return _NC_CACHE["nc"]


def kernel(L0_attn_x, L0_attn_W_K, L0_attn_W_V, L0_attn_W1, L0_attn_W2,
           L0_attn_m1, L0_attn_m2, L0_ffn_w_gate, L0_ffn_w_up, L0_ffn_w_down):
    x = np.ascontiguousarray(np.asarray(L0_attn_x, dtype=np.float32)).reshape(NCORES * NTOK, D)
    wkN = np.ascontiguousarray(np.asarray(L0_attn_W_K, dtype=np.float32))
    w1T = np.ascontiguousarray(np.asarray(L0_attn_W1, dtype=np.float32).T)
    w2T = np.ascontiguousarray(np.asarray(L0_attn_W2, dtype=np.float32).T)
    wgT = np.ascontiguousarray(np.asarray(L0_ffn_w_gate, dtype=np.int32).T)
    wuT = np.ascontiguousarray(np.asarray(L0_ffn_w_up, dtype=np.int32).T)
    wdT = np.ascontiguousarray(np.asarray(L0_ffn_w_down, dtype=np.int32).T)
    idk = np.eye(128, dtype=np.float32) * np.float32(INVKAPPA)

    nc = _get_nc()
    in_maps = []
    for i in range(NCORES):
        xTi = np.ascontiguousarray(x[i * NTOK:(i + 1) * NTOK, :].T)
        in_maps.append({"xT": xTi, "wkN": wkN, "w1T": w1T, "w2T": w2T,
                        "wgT": wgT, "wuT": wuT, "wdT": wdT, "idk": idk})
    res = run_bass_kernel_spmd(nc, in_maps, core_ids=list(range(NCORES)))
    outs = []
    for i in range(NCORES):
        outs.append(np.ascontiguousarray(np.asarray(res.results[i]["outT"]).T))
    return np.concatenate(outs, axis=0).reshape(NCORES, NTOK, D).astype(np.float32, copy=False)


# version 50
# speedup vs baseline: 1.1866x; 1.1866x over previous
import numpy as np
import contextlib
import concourse.bass as bass
import concourse.mybir as mybir
from concourse.bass_utils import run_bass_kernel_spmd
from concourse.alu_op_type import AluOpType

F32 = mybir.dt.float32
BF16 = mybir.dt.bfloat16
I32 = mybir.dt.int32
AF = mybir.ActivationFunctionType
OP = AluOpType

NCORES = 8
D = 1024
DT = 8
DH = 64
H = 3072
HT = 24
NTOK = 4096
TC = 256
NCH = NTOK // TC
KAPPA = float(1.0 / 127.0 ** 2)
INVKAPPA = float(127.0 ** 2)
INV127 = float(1.0 / 127.0)
INV_SQRT2 = 0.7071067811865476
EPS = 1e-5
INVD = float(1.0 / D)

_FFN_SEQ = []
for _j in range(HT):
    _FFN_SEQ.append(("G", _j))
    if _j >= 1:
        _FFN_SEQ.append(("D", _j - 1))
_FFN_SEQ.append(("D", HT - 1))

PE_SLOTS = {
    0: ["ST2a"], 1: ["ST2b"], 2: ["BC2"],
    5: ["H"], 7: ["R0"], 8: ["R1"], 9: ["R2"], 10: ["R3"],
    12: ["ST1a"], 13: ["ST1b"], 14: ["BC1"],
}
DVE_SLOTS = {
    0: ["Q2_45"], 1: ["Q2_67"], 2: ["FIN2"], 3: ["BCT2"],
    8: ["SE0"], 9: ["SE1", "STQ01"], 10: ["SE2", "STQ23"],
    11: ["SE3", "STQ45"], 12: ["STQ67", "FIN1"],
    14: ["BCT1"], 16: ["Y1_01"], 17: ["Y1_23"], 18: ["Y1_45", "Y1_67"],
}
GP_SEQ = ["SUB01", "SUB23", "SUB45", "SUB67",
          "MUL01", "MUL23", "MUL45", "MUL67", "XL"]
SC_SLOTS = {
    5: ["GELU"],
    7: ["RS2"],
    15: ["RS1"],
}


class WaitTracker:
    def __init__(self):
        self.seen = {}

    def need(self, sem, val):
        if val is None or val <= 0:
            return False
        if self.seen.get(sem.name, -1) >= val:
            return False
        self.seen[sem.name] = val
        return True


class Plan:
    def __init__(self):
        self.ev = {}
        self.cnt = {}
        self.recording = True
        self.trackers = {}
        self.prog = {}
        self.cur = None
        self.chain_vals = []
        self.chain_idx = 0

    def begin_stream(self, name):
        self.cur = self.prog.setdefault(name, [])

    def inc(self, ins, sem, key=None, amount=1):
        if self.recording:
            c = self.cnt.get(sem.name, 0) + amount
            self.cnt[sem.name] = c
            if key is not None:
                assert key not in self.ev, key
                self.ev[key] = c
            self.cur.append(("inc", sem.name, amount, key))
        else:
            ins.then_inc(sem, amount)

    def chain(self, ins, eng, sem):
        if self.recording:
            c = self.cnt.get(sem.name, 0) + 1
            self.cnt[sem.name] = c
            self.chain_vals.append(c)
            self.cur.append(("inc", sem.name, 1, None))
        else:
            ins.then_inc(sem, 1)
            eng.wait_ge(sem, self.chain_vals[self.chain_idx])
            self.chain_idx += 1

    def mark(self, eng, sem):
        if self.recording:
            self.chain_vals.append(self.cnt.get(sem.name, 0))
        else:
            val = self.chain_vals[self.chain_idx]
            self.chain_idx += 1
            if val > 0:
                eng.wait_ge(sem, val)

    def wait(self, eng, sem, key):
        if self.recording:
            if isinstance(key, int):
                if key > 0:
                    self.cur.append(("wait", sem.name, key, None))
            else:
                self.cur.append(("wait", sem.name, None, key))
            return
        if isinstance(key, int):
            val = key
        else:
            val = self.ev.get(key)
            if val is None:
                assert not (0 <= key[1] <= NCH - 1), f"missing event {key}"
                return
        tr = self.trackers.setdefault(id(eng), WaitTracker())
        if tr.need(sem, val):
            eng.wait_ge(sem, val)

    def verify_no_deadlock(self):
        pos = {k: 0 for k in self.prog}
        sems = {}
        progress = True
        while progress:
            progress = False
            for name, ops in self.prog.items():
                i = pos[name]
                while i < len(ops):
                    op, semn, val, key = ops[i]
                    if op == "wait":
                        if key is not None:
                            val = self.ev.get(key)
                        if val is not None and sems.get(semn, 0) < val:
                            break
                    else:
                        sems[semn] = sems.get(semn, 0) + val
                        progress = progress or True
                    i += 1
                    progress = True
                pos[name] = i
        stuck = {k: (pos[k], len(v)) for k, v in self.prog.items()
                 if pos[k] < len(v)}
        if stuck:
            detail = {}
            for k in stuck:
                op = self.prog[k][pos[k]]
                detail[k] = op
            raise AssertionError(f"plan deadlock: {stuck} blocked at {detail} "
                                 f"sems={sems}")


class FakeIns:
    def then_inc(self, *a, **k):
        return self


class FakeEng:
    def __getattr__(self, name):
        return lambda *a, **k: FakeIns()

    def wait_ge(self, *a, **k):
        pass


def build_nc():
    nc = bass.Bass("TRN2", target_bir_lowering=False)

    xT = nc.dram_tensor("xT", [D, NTOK], F32, kind="ExternalInput")
    wkN = nc.dram_tensor("wkN", [D, D], F32, kind="ExternalInput")
    w1T = nc.dram_tensor("w1T", [D, DH], F32, kind="ExternalInput")
    w2T = nc.dram_tensor("w2T", [DH, D], F32, kind="ExternalInput")
    wgT = nc.dram_tensor("wgT", [D, H], I32, kind="ExternalInput")
    wuT = nc.dram_tensor("wuT", [D, H], I32, kind="ExternalInput")
    wdT = nc.dram_tensor("wdT", [H, D], I32, kind="ExternalInput")
    idk = nc.dram_tensor("idk", [128, 128], F32, kind="ExternalInput")
    outT = nc.dram_tensor("outT", [D, NTOK], F32, kind="ExternalOutput")

    xTr = xT.rearrange("(a p) t -> p a t", p=128)
    outTr = outT.rearrange("(a p) t -> p a t", p=128)

    es = contextlib.ExitStack()
    def sb(name, shape, dtype):
        return es.enter_context(nc.sbuf_tensor(name, shape, dtype))
    def psum(name):
        return es.enter_context(nc.psum_tensor(name, [128, 512], F32))
    def sem(name):
        return es.enter_context(nc.semaphore(name))

    with es:
        WK = sb("WK", [128, DT, D], BF16)
        W1 = sb("W1", [128, DT, DH], BF16)
        W2 = sb("W2", [DH, D], BF16)
        WG = sb("WG", [128, DT, H], BF16)
        WU = sb("WU", [128, DT, H], BF16)
        WD = sb("WD", [128, HT, D], BF16)
        IDK = sb("IDK", [128, 128], BF16)
        XF = sb("XF", [128, 3, DT, TC], BF16)
        OUTB = sb("OUTB", [128, 4, TC], F32)
        Y1 = sb("Y1", [128, 2, DT, TC], BF16)
        SQT = sb("SQT", [128, 4, TC], BF16)
        SG = sb("SG", [128, 4, TC], BF16)
        H2 = sb("H2", [128, 4, TC], BF16)
        AG = sb("AG", [DH, TC], BF16)
        ONES = sb("ONES", [128, 1], BF16)
        ONESROW = sb("ONESROW", [1, 128], BF16)
        EPST = sb("EPST", [128, 1], F32)
        SROW = sb("SROW", [1, 512], BF16)
        MUT = sb("MUT", [128, TC], F32)
        VART = sb("VART", [128, TC], F32)
        MSQT = sb("MSQT", [128, TC], F32)
        RT = sb("RT", [128, TC], F32)
        MUT2 = sb("MUT2", [128, TC], F32)
        VART2 = sb("VART2", [128, TC], F32)
        RT2 = sb("RT2", [128, TC], F32)
        MSQT2 = sb("MSQT2", [128, TC], F32)
        SUBT = sb("SUBT", [128, DT, TC], BF16)

        P = [psum(f"P{i}") for i in range(8)]
        PDB = [P[0], P[1], P[2], P[7]]
        PGU = [P[3], P[4]]
        def KB(g): return P[5] if g % 2 == 0 else P[6]
        def RB(g): return P[6] if g % 2 == 0 else P[5]
        PH = P[5]
        PST1, PBC1 = P[6], P[5]
        PST2, PBC2 = P[5], P[6]

        s_t = sem("s_t")
        s_v = sem("s_v")
        s_s = sem("s_s")
        s_x3 = [sem(f"s_x{k}") for k in range(3)]
        s_ob = [sem(f"s_ob{k}") for k in range(4)]
        s_wk = sem("s_wk")
        s_w1 = sem("s_w1")
        s_w2 = sem("s_w2")
        s_id = sem("s_id")
        s_g = sem("s_g")
        s_wg = [sem(f"s_wg{q}") for q in range(8)]
        s_wu = [sem(f"s_wu{q}") for q in range(8)]
        s_wd = [sem(f"s_wd{q}") for q in range(8)]

        E_ = Plan()

        def attn_item(t, item, c):
            if item == "H":
                E_.wait(t, s_x3[c % 3], 16 * (c // 3 + 1))
                E_.wait(t, s_v, ("v", -1, "W1K"))
                E_.wait(t, s_v, ("v", c - 2, "FIN2"))
                for a in range(DT):
                    ins = t.matmul(PH[0:DH, 0:TC], W1[:, a, :],
                                   XF[:, c % 3, a, :],
                                   start=(a == 0), stop=(a == DT - 1))
                E_.inc(ins, s_t, ("t", c, "H"))
            elif item.startswith("R"):
                g = int(item[1])
                bank = RB(g)
                if c == 0:
                    E_.wait(t, s_w2, 16)
                E_.wait(t, s_s, ("s", c, "AG"))
                if g == 0:
                    E_.wait(t, s_v, ("v", c - 2, "BCT2"))
                elif g == 1:
                    pass
                else:
                    E_.wait(t, s_v, ("v", c, f"SE{g - 2}"))
                for oh in range(2):
                    o = 2 * g + oh
                    ins = t.matmul(bank[:, oh * TC:(oh + 1) * TC],
                                   W2[0:DH, o * 128:(o + 1) * 128],
                                   AG[0:DH, :], start=(oh == 0), stop=(oh == 1))
                E_.inc(ins, s_t, ("t", c, f"R{g}"))
            elif item in ("ST1a", "ST1b"):
                lo = 0 if item == "ST1a" else 4
                if lo == 0:
                    E_.wait(t, s_v, ("v", c, "SE2"))
                for a in range(lo, lo + 4):
                    E_.wait(t, s_v, ("v", c, f"SE{a // 2}"))
                    t.matmul(PST1[0:1, 0:TC], ONES[:, 0:1],
                             XF[:, c % 3, a, :],
                             start=(a == 0), stop=False)
                    E_.wait(t, s_v, ("v", c, f"STQ{a}"))
                    E_.inc(t.matmul(PST1[0:1, TC:2 * TC], ONES[:, 0:1],
                                    SQT[:, a % 4, :],
                                    start=False, stop=(a == DT - 1)),
                           s_t, ("t", c, f"ST1_{a}"))
            elif item == "BC1":
                E_.wait(t, s_v, ("v", c, "SE3"))
                E_.wait(t, s_v, ("v", c, "FIN1"))
                E_.inc(t.matmul(PBC1[:, 0:512], ONESROW[0:1, 0:128],
                                SROW[0:1, 0:512], start=True, stop=True),
                       s_t, ("t", c, "BC1"))
            elif item in ("ST2a", "ST2b"):
                lo = 0 if item == "ST2a" else 4
                if lo == 0:
                    E_.wait(t, s_v, ("v", c + 1, "BCT1"))
                for a in range(lo, lo + 4):
                    E_.wait(t, s_s, ("s", c, f"DE{a // 2}"))
                    t.matmul(PST2[0:1, 0:TC], ONES[:, 0:1],
                             XF[:, c % 3, a, :],
                             start=(a == 0), stop=False)
                    E_.wait(t, s_v, ("v", c, f"SQ2_{a}"))
                    E_.inc(t.matmul(PST2[0:1, TC:2 * TC], ONES[:, 0:1],
                                    SQT[:, a % 4, :],
                                    start=False, stop=(a == DT - 1)),
                           s_t, ("t", c, f"ST2_{a}"))
            elif item == "BC2":
                E_.wait(t, s_v, ("v", c + 1, "FIN1"))
                E_.wait(t, s_v, ("v", c, "FIN2"))
                E_.inc(t.matmul(PBC2[:, 0:512], ONESROW[0:1, 0:128],
                                SROW[0:1, 0:512], start=True, stop=True),
                       s_t, ("t", c, "BC2"))
            else:
                raise AssertionError(item)

        def emit_tensor(t):
            E_.wait(t, s_wk, 16)
            E_.wait(t, s_w1, 16)
            for hb in range(DT):
                for kb in range(DT):
                    ins = t.matmul(P[5][:, hb * DH:(hb + 1) * DH],
                                   WK[:, kb, hb * 128:(hb + 1) * 128],
                                   W1[:, kb, :],
                                   start=(hb == 0 and kb == 0),
                                   stop=(hb == DT - 1 and kb == DT - 1))
            E_.inc(ins, s_t, ("t", -1, "W1K"))
            for item in ["H", "R0", "R1", "R2", "R3", "ST1a", "ST1b", "BC1"]:
                attn_item(t, item, 0)
            for i in range(NCH + 1):
                if i <= NCH - 1:
                    for kind, j in _FFN_SEQ:
                        if kind == "G":
                            bank = PGU[j % 2]
                            if i == 0:
                                E_.wait(t, s_wg[j // 3], 16)
                                E_.wait(t, s_wu[j // 3], 16)
                            if j >= 2:
                                E_.wait(t, s_v, ("v", i, f"H2_{j - 2}"))
                            else:
                                E_.wait(t, s_v, ("v", i, "Y1_3"))
                                E_.wait(t, s_v, ("v", i, "Y1_7"))
                                E_.wait(t, s_v, ("v", i - 1, f"H2_{22 + j}"))
                            for a in range(DT):
                                ins = t.matmul(bank[:, 0:TC],
                                               WG[:, a, j * 128:(j + 1) * 128],
                                               Y1[:, i % 2, a, :], start=(a == 0),
                                               stop=False)
                            for a in range(DT):
                                ins = t.matmul(bank[:, TC:2 * TC],
                                               WU[:, a, j * 128:(j + 1) * 128],
                                               Y1[:, i % 2, a, :], start=False,
                                               stop=(a == DT - 1))
                            E_.inc(ins, s_t, ("t", i, f"TG{j}"))
                        else:
                            if i == 0:
                                E_.wait(t, s_wd[j // 3], 16)
                                if j == 0:
                                    E_.wait(t, s_id, 16)
                            for item in PE_SLOTS.get(j, []):
                                if item in ("ST2a", "ST2b", "BC2"):
                                    if i >= 1:
                                        attn_item(t, item, i - 1)
                                elif i + 1 <= NCH - 1:
                                    attn_item(t, item, i + 1)
                            E_.wait(t, s_v, ("v", i, f"H2_{j}"))
                            if j == 0:
                                E_.wait(t, s_s, ("s", i - 1, "DE3"))
                                for o in range(DT):
                                    t.matmul(PDB[o // 2][:, (o % 2) * TC:(o % 2 + 1) * TC],
                                             IDK[:, :], Y1[:, i % 2, o, :],
                                             start=(o % 2 == 0), stop=False)
                            for o in range(DT):
                                ins = t.matmul(PDB[o // 2][:, (o % 2) * TC:(o % 2 + 1) * TC],
                                               WD[:, j, o * 128:(o + 1) * 128],
                                               H2[:, j % 4, :],
                                               start=False,
                                               stop=(j == HT - 1 and o % 2 == 1))
                            E_.inc(ins, s_t, ("t", i, f"TD{j}"))
                else:
                    for item in ("ST2a", "ST2b", "BC2"):
                        attn_item(t, item, i - 1)

        def dve_item(v, item, i):
            if item.startswith("SE"):
                c, g = i + 1, int(item[2])
                E_.wait(v, s_x3[c % 3], 16 * (c // 3 + 1))
                E_.wait(v, s_t, ("t", c, f"R{g}"))
                E_.inc(v.tensor_tensor(XF[:, c % 3, 2 * g:2 * g + 2, :],
                                       XF[:, c % 3, 2 * g:2 * g + 2, :],
                                       RB(g)[:, 0:512], OP.add),
                       s_v, ("v", c, f"SE{g}"))
            elif item == "FIN1":
                c = i + 1
                E_.wait(v, s_t, ("t", c, "ST1_7"))
                E_.wait(v, s_t, ("t", c - 2, "BC2"))
                E_.inc(v.tensor_copy(SROW[0:1, 0:512], PST1[0:1, 0:512]),
                       s_v, ("v", c, "FIN1"))
            elif item == "BCT1":
                c = i + 1
                E_.wait(v, s_t, ("t", c, "BC1"))
                E_.wait(v, s_v, ("v", c - 1, "Y1_7"))
                E_.chain(v.tensor_scalar(MUT[:], PBC1[:, 0:TC], INVD, None, OP.mult),
                         v, s_v)
                E_.chain(v.scalar_tensor_tensor(MSQT[:], MUT[:], -1.0, MUT[:],
                                                OP.mult, OP.mult), v, s_v)
                E_.inc(v.scalar_tensor_tensor(VART[:], PBC1[:, TC:2 * TC], INVD,
                                              MSQT[:], OP.mult, OP.add),
                       s_v, ("v", c, "BCT1"))
            elif item.startswith("Y1_"):
                c = i + 1
                E_.wait(v, s_v, ("v", c, "SE3"))
                E_.wait(v, s_s, ("s", c, "RS1"))
                E_.wait(v, s_t, ("t", c - 2, "TG23"))
                for a in (int(item[3]), int(item[4])):
                    E_.chain(v.tensor_tensor(XF[:, c % 3, a, :], XF[:, c % 3, a, :],
                                             MUT[:], OP.subtract), v, s_v)
                    E_.inc(v.tensor_tensor(Y1[:, c % 2, a, :], XF[:, c % 3, a, :],
                                           RT[:], OP.mult),
                           s_v, ("v", c, f"Y1_{a}"))
            elif item == "FIN2":
                c = i - 1
                E_.wait(v, s_t, ("t", c, "ST2_7"))
                E_.wait(v, s_t, ("t", c + 1, "BC1"))
                E_.inc(v.tensor_copy(SROW[0:1, 0:512], PST2[0:1, 0:512]),
                       s_v, ("v", c, "FIN2"))
            elif item == "BCT2":
                c = i - 1
                E_.wait(v, s_t, ("t", c, "BC2"))
                E_.wait(v, s_g, ("g", c - 1, "SUB7"))
                E_.chain(v.tensor_scalar(MUT2[:], PBC2[:, 0:TC], INVD, None, OP.mult),
                         v, s_v)
                E_.chain(v.scalar_tensor_tensor(MSQT[:], MUT2[:], -1.0, MUT2[:],
                                                OP.mult, OP.mult), v, s_v)
                E_.inc(v.scalar_tensor_tensor(VART2[:], PBC2[:, TC:2 * TC], INVD,
                                              MSQT[:], OP.mult, OP.add),
                       s_v, ("v", c, "BCT2"))
            elif item.startswith("STQ"):
                c = i + 1
                for a in (int(item[3]), int(item[4])):
                    E_.wait(v, s_v, ("v", c, f"SE{a // 2}"))
                    if a < 4:
                        E_.wait(v, s_t, ("t", c - 2, f"ST2_{4 + a}"))
                    else:
                        E_.wait(v, s_t, ("t", c, f"ST1_{a - 4}"))
                    E_.inc(v.tensor_tensor(SQT[:, a % 4, :],
                                           XF[:, c % 3, a, :],
                                           XF[:, c % 3, a, :], OP.mult),
                           s_v, ("v", c, f"STQ{a}"))
            elif item.startswith("Q2_"):
                c = i - 1
                for a in (int(item[3]), int(item[4])):
                    E_.wait(v, s_s, ("s", c, f"DE{a // 2}"))
                    if a < 4:
                        E_.wait(v, s_t, ("t", c + 1, f"ST1_{4 + a}"))
                    else:
                        E_.wait(v, s_t, ("t", c, f"ST2_{a - 4}"))
                    E_.inc(v.tensor_tensor(SQT[:, a % 4, :],
                                           XF[:, c % 3, a, :],
                                           XF[:, c % 3, a, :], OP.mult),
                           s_v, ("v", c, f"SQ2_{a}"))
            else:
                raise AssertionError(item)

        def emit_vector(v):
            v.memset(ONES[:], 1.0)
            v.memset(EPST[:], EPS)
            E_.inc(v.memset(ONESROW[:], 1.0), s_v, ("v", -1, "INIT"))
            E_.wait(v, s_t, ("t", -1, "W1K"))
            E_.inc(v.tensor_copy(W1[:, :, :], P[5][:, 0:512]),
                   s_v, ("v", -1, "W1K"))
            for item in ["SE0", "STQ01", "SE1", "STQ23", "SE2", "SE3",
                         "STQ45", "STQ67", "FIN1", "BCT1", "Y1_01", "Y1_23",
                         "Y1_45", "Y1_67"]:
                dve_item(v, item, -1)
            for i in range(NCH + 1):
                if i <= NCH - 1:
                    for j in range(HT):
                        E_.wait(v, s_s, ("s", i, f"SIG{j}"))
                        if j >= 4:
                            E_.wait(v, s_t, ("t", i, f"TD{j - 4}"))
                        else:
                            E_.wait(v, s_t, ("t", i - 1, f"TD{20 + j}"))
                        bank = PGU[j % 2]
                        E_.inc(v.tensor_tensor(H2[:, j % 4, :], SG[:, j % 4, :],
                                               bank[:, TC:2 * TC], OP.mult),
                               s_v, ("v", i, f"H2_{j}"))
                        for item in DVE_SLOTS.get(j, []):
                            if item in ("FIN2", "BCT2") or item.startswith("Q2_"):
                                if i >= 1:
                                    dve_item(v, item, i)
                            elif i + 1 <= NCH - 1:
                                dve_item(v, item, i)
                    dve_item(v, "Q2_01", i + 1)
                    dve_item(v, "Q2_23", i + 1)
                else:
                    for item in ["Q2_45", "Q2_67", "FIN2", "BCT2"]:
                        dve_item(v, item, i)
                    c = NCH - 1
                    E_.wait(v, s_v, ("v", c, "BCT2"))
                    E_.wait(v, s_g, ("g", c - 1, "OUT7"))
                    for a in range(DT):
                        E_.inc(v.tensor_tensor(SUBT[:, a, :], XF[:, c % 3, a, :],
                                               MUT2[:], OP.subtract),
                               s_v, ("v", c, f"SUB{a}"))
                    E_.wait(v, s_s, ("s", c, "RS2"))
                    for a in range(DT):
                        n = 8 * c + a
                        E_.wait(v, s_ob[n % 4], 16 * (n // 4))
                        E_.inc(v.tensor_tensor(OUTB[:, a % 4, :], SUBT[:, a, :],
                                               RT2[:], OP.mult),
                               s_v, ("v", c, f"OUT{a}"))

        def sc_item(sc, item, i):
            if item == "GELU":
                c = i + 1
                E_.wait(sc, s_t, ("t", c, "H"))
                E_.wait(sc, s_t, ("t", c - 1, "R3"))
                E_.inc(sc.activation(AG[0:DH, :], PH[0:DH, 0:TC], AF.Gelu),
                       s_s, ("s", c, "AG"))
            elif item == "RS1":
                c = i + 1
                E_.wait(sc, s_v, ("v", c, "BCT1"))
                E_.wait(sc, s_v, ("v", c - 1, "Y1_7"))
                E_.chain(sc.activation(MSQT2[:], VART[:], AF.Ln, bias=EPST[:, 0:1]),
                         sc, s_s)
                E_.inc(sc.activation(RT[:], MSQT2[:], AF.Exp, scale=-0.5),
                       s_s, ("s", c, "RS1"))
            elif item == "RS2":
                c = i - 1
                E_.wait(sc, s_v, ("v", c, "BCT2"))
                E_.wait(sc, s_g, ("g", c - 1, "OUT7"))
                E_.chain(sc.activation(MSQT2[:], VART2[:], AF.Ln, bias=EPST[:, 0:1]),
                         sc, s_s)
                E_.inc(sc.activation(RT2[:], MSQT2[:], AF.Exp, scale=-0.5),
                       s_s, ("s", c, "RS2"))
            elif item.startswith("DE"):
                c, g = i, int(item[2])
                E_.wait(sc, s_t, ("t", c, "TD23"))
                E_.inc(sc.activation(XF[:, c % 3, 2 * g:2 * g + 2, :],
                                     PDB[g][:, 0:512], AF.Copy, scale=KAPPA),
                       s_s, ("s", c, f"DE{g}"))
            else:
                raise AssertionError(item)

        def emit_scalar(sc):
            for item in ["GELU", "RS1"]:
                sc_item(sc, item, -1)
            for i in range(NCH + 1):
                if i <= NCH - 1:
                    for j in range(HT):
                        E_.wait(sc, s_t, ("t", i, f"TG{j}"))
                        if j >= 4:
                            E_.wait(sc, s_v, ("v", i, f"H2_{j - 4}"))
                        else:
                            E_.wait(sc, s_v, ("v", i - 1, f"H2_{20 + j}"))
                        E_.inc(sc.activation(SG[:, j % 4, :], PGU[j % 2][:, 0:TC],
                                             AF.Silu, scale=INV127),
                               s_s, ("s", i, f"SIG{j}"))
                        for item in SC_SLOTS.get(j, []):
                            if item == "RS2":
                                if i >= 1:
                                    sc_item(sc, item, i)
                            elif i + 1 <= NCH - 1:
                                sc_item(sc, item, i)
                    for g in range(4):
                        sc_item(sc, f"DE{g}", i)
                else:
                    sc_item(sc, "RS2", i)

        def gp_item(g, item, i):
            if item == "XL":
                c = i + 2
                E_.inc(g.dma_start(XF[:, c % 3], xTr[:, :, c * TC:(c + 1) * TC]),
                       s_x3[c % 3], None, 16)
            elif item.startswith("SUB"):
                c = i - 1
                E_.wait(g, s_v, ("v", c, "BCT2"))
                E_.mark(g, s_g)
                for a in (int(item[3]), int(item[4])):
                    E_.inc(g.tensor_tensor(SUBT[:, a, :], XF[:, c % 3, a, :],
                                           MUT2[:], OP.subtract),
                           s_g, ("g", c, f"SUB{a}"))
            elif item.startswith("MUL"):
                c = i - 1
                E_.wait(g, s_s, ("s", c, "RS2"))
                E_.mark(g, s_g)
                for a in (int(item[3]), int(item[4])):
                    n = 8 * c + a
                    if n >= 4:
                        E_.wait(g, s_ob[n % 4], 16 * (n // 4))
                    E_.inc(g.tensor_tensor(OUTB[:, a % 4, :], SUBT[:, a, :],
                                           RT2[:], OP.mult), s_g, ("g", c, f"OUT{a}"))
            else:
                raise AssertionError(item)

        def emit_gpsimd(g):
            wgr = wgT.rearrange("(a p) h -> p a h", p=128)
            wur = wuT.rearrange("(a p) h -> p a h", p=128)
            wdr = wdT.rearrange("(j p) o -> p j o", p=128)
            HQ = H // 8
            JQ = HT // 8
            E_.inc(g.dma_start(WK[:], wkN.rearrange("(kb p) h -> p kb h", p=128)),
                   s_wk, None, 16)
            E_.inc(g.dma_start(W1[:], w1T.rearrange("(a p) m -> p a m", p=128)),
                   s_w1, None, 16)
            E_.inc(g.dma_start(XF[:, 0], xTr[:, :, 0:TC]), s_x3[0], None, 16)
            E_.inc(g.dma_start(W2[:], w2T[:, :]), s_w2, None, 16)
            E_.inc(g.dma_start(IDK[:], idk[:, :]), s_id, None, 16)
            for q in range(8):
                E_.inc(g.dma_start(WG[:, :, q * HQ:(q + 1) * HQ],
                                   wgr[:, :, q * HQ:(q + 1) * HQ]), s_wg[q], None, 16)
                E_.inc(g.dma_start(WU[:, :, q * HQ:(q + 1) * HQ],
                                   wur[:, :, q * HQ:(q + 1) * HQ]), s_wu[q], None, 16)
                E_.inc(g.dma_start(WD[:, q * JQ:(q + 1) * JQ, :],
                                   wdr[:, q * JQ:(q + 1) * JQ, :]), s_wd[q], None, 16)
                if q == 0:
                    E_.inc(g.dma_start(XF[:, 1], xTr[:, :, TC:2 * TC]),
                           s_x3[1], None, 16)


            for i in range(NCH + 1):
                for item in GP_SEQ:
                    if item == "XL":
                        if i + 2 <= NCH - 1:
                            gp_item(g, item, i)
                    elif item.startswith("Y1_"):
                        if i + 1 <= NCH - 1:
                            gp_item(g, item, i)
                    else:
                        if 1 <= i <= NCH - 1:
                            gp_item(g, item, i)

        def emit_sync(s):
            for c in range(NCH):
                for a in range(DT):
                    n = 8 * c + a
                    if c == NCH - 1:
                        E_.wait(s, s_v, ("v", c, f"OUT{a}"))
                    else:
                        E_.wait(s, s_g, ("g", c, f"OUT{a}"))
                    E_.inc(s.dma_start(outTr[:, a, c * TC:(c + 1) * TC],
                                       OUTB[:, a % 4, :]), s_ob[n % 4], None, 16)
            if not E_.recording:
                for k in range(4):
                    s.wait_ge(s_ob[k], 16 * (8 * NCH // 4))

        fe = FakeEng()
        E_.begin_stream("g"); emit_gpsimd(fe)
        E_.begin_stream("q"); emit_sync(fe)
        E_.begin_stream("t"); emit_tensor(fe)
        E_.begin_stream("v"); emit_vector(fe)
        E_.begin_stream("s"); emit_scalar(fe)
        E_.verify_no_deadlock()
        E_.recording = False

        block = es.enter_context(nc.Block())

        @block.gpsimd
        def _(g):
            emit_gpsimd(g)

        @block.sync
        def _(s):
            emit_sync(s)

        @block.tensor
        def _(t):
            emit_tensor(t)

        @block.vector
        def _(v):
            emit_vector(v)

        @block.scalar
        def _(sc):
            emit_scalar(sc)

    return nc


_NC_CACHE = {}

def _get_nc():
    if "nc" not in _NC_CACHE:
        _NC_CACHE["nc"] = build_nc()
    return _NC_CACHE["nc"]


def kernel(L0_attn_x, L0_attn_W_K, L0_attn_W_V, L0_attn_W1, L0_attn_W2,
           L0_attn_m1, L0_attn_m2, L0_ffn_w_gate, L0_ffn_w_up, L0_ffn_w_down):
    x = np.ascontiguousarray(np.asarray(L0_attn_x, dtype=np.float32)).reshape(NCORES * NTOK, D)
    wkN = np.ascontiguousarray(np.asarray(L0_attn_W_K, dtype=np.float32))
    w1T = np.ascontiguousarray(np.asarray(L0_attn_W1, dtype=np.float32).T)
    w2T = np.ascontiguousarray(np.asarray(L0_attn_W2, dtype=np.float32).T)
    wgT = np.ascontiguousarray(np.asarray(L0_ffn_w_gate, dtype=np.int32).T)
    wuT = np.ascontiguousarray(np.asarray(L0_ffn_w_up, dtype=np.int32).T)
    wdT = np.ascontiguousarray(np.asarray(L0_ffn_w_down, dtype=np.int32).T)
    idk = np.eye(128, dtype=np.float32) * np.float32(INVKAPPA)

    nc = _get_nc()
    in_maps = []
    for i in range(NCORES):
        xTi = np.ascontiguousarray(x[i * NTOK:(i + 1) * NTOK, :].T)
        in_maps.append({"xT": xTi, "wkN": wkN, "w1T": w1T, "w2T": w2T,
                        "wgT": wgT, "wuT": wuT, "wdT": wdT, "idk": idk})
    res = run_bass_kernel_spmd(nc, in_maps, core_ids=list(range(NCORES)))
    outs = []
    for i in range(NCORES):
        outs.append(np.ascontiguousarray(np.asarray(res.results[i]["outT"]).T))
    return np.concatenate(outs, axis=0).reshape(NCORES, NTOK, D).astype(np.float32, copy=False)


# version 51 (final)
# speedup vs baseline: 1.1869x; 1.0002x over previous
import numpy as np
import contextlib
import concourse.bass as bass
import concourse.mybir as mybir
from concourse.bass_utils import run_bass_kernel_spmd
from concourse.alu_op_type import AluOpType

F32 = mybir.dt.float32
BF16 = mybir.dt.bfloat16
I32 = mybir.dt.int32
AF = mybir.ActivationFunctionType
OP = AluOpType

NCORES = 8
D = 1024
DT = 8
DH = 64
H = 3072
HT = 24
NTOK = 4096
TC = 256
NCH = NTOK // TC
KAPPA = float(1.0 / 127.0 ** 2)
INVKAPPA = float(127.0 ** 2)
INV127 = float(1.0 / 127.0)
INV_SQRT2 = 0.7071067811865476
EPS = 1e-5
INVD = float(1.0 / D)

_FFN_SEQ = []
for _j in range(HT):
    _FFN_SEQ.append(("G", _j))
    if _j >= 1:
        _FFN_SEQ.append(("D", _j - 1))
_FFN_SEQ.append(("D", HT - 1))

PE_SLOTS = {
    0: ["ST2a"], 1: ["ST2b"], 2: ["BC2"],
    5: ["H"], 7: ["R0"], 8: ["R1"], 9: ["R2"], 10: ["R3"],
    12: ["ST1a"], 13: ["ST1b"], 14: ["BC1"],
}
DVE_SLOTS = {
    0: ["Q2_45"], 1: ["Q2_67"], 2: ["FIN2"], 3: ["BCT2"],
    8: ["SE0"], 9: ["SE1", "STQ01"], 10: ["SE2", "STQ23"],
    11: ["SE3", "STQ45"], 12: ["STQ67", "FIN1"],
    14: ["BCT1"], 16: ["Y1_01"], 17: ["Y1_23"], 18: ["Y1_45", "Y1_67"],
}
GP_SEQ = ["SUB01", "SUB23", "SUB45", "SUB67",
          "MUL01", "MUL23", "MUL45", "MUL67", "XL"]
SC_SLOTS = {
    5: ["GELU"],
    7: ["RS2"],
    15: ["RS1"],
}


class WaitTracker:
    def __init__(self):
        self.seen = {}

    def need(self, sem, val):
        if val is None or val <= 0:
            return False
        if self.seen.get(sem.name, -1) >= val:
            return False
        self.seen[sem.name] = val
        return True


class Plan:
    def __init__(self):
        self.ev = {}
        self.cnt = {}
        self.recording = True
        self.trackers = {}
        self.prog = {}
        self.cur = None
        self.chain_vals = []
        self.chain_idx = 0

    def begin_stream(self, name):
        self.cur = self.prog.setdefault(name, [])

    def inc(self, ins, sem, key=None, amount=1):
        if self.recording:
            c = self.cnt.get(sem.name, 0) + amount
            self.cnt[sem.name] = c
            if key is not None:
                assert key not in self.ev, key
                self.ev[key] = c
            self.cur.append(("inc", sem.name, amount, key))
        else:
            ins.then_inc(sem, amount)

    def chain(self, ins, eng, sem):
        if self.recording:
            c = self.cnt.get(sem.name, 0) + 1
            self.cnt[sem.name] = c
            self.chain_vals.append(c)
            self.cur.append(("inc", sem.name, 1, None))
        else:
            ins.then_inc(sem, 1)
            eng.wait_ge(sem, self.chain_vals[self.chain_idx])
            self.chain_idx += 1

    def mark(self, eng, sem):
        if self.recording:
            self.chain_vals.append(self.cnt.get(sem.name, 0))
        else:
            val = self.chain_vals[self.chain_idx]
            self.chain_idx += 1
            if val > 0:
                eng.wait_ge(sem, val)

    def wait(self, eng, sem, key):
        if self.recording:
            if isinstance(key, int):
                if key > 0:
                    self.cur.append(("wait", sem.name, key, None))
            else:
                self.cur.append(("wait", sem.name, None, key))
            return
        if isinstance(key, int):
            val = key
        else:
            val = self.ev.get(key)
            if val is None:
                assert not (0 <= key[1] <= NCH - 1), f"missing event {key}"
                return
        tr = self.trackers.setdefault(id(eng), WaitTracker())
        if tr.need(sem, val):
            eng.wait_ge(sem, val)

    def verify_no_deadlock(self):
        pos = {k: 0 for k in self.prog}
        sems = {}
        progress = True
        while progress:
            progress = False
            for name, ops in self.prog.items():
                i = pos[name]
                while i < len(ops):
                    op, semn, val, key = ops[i]
                    if op == "wait":
                        if key is not None:
                            val = self.ev.get(key)
                        if val is not None and sems.get(semn, 0) < val:
                            break
                    else:
                        sems[semn] = sems.get(semn, 0) + val
                        progress = progress or True
                    i += 1
                    progress = True
                pos[name] = i
        stuck = {k: (pos[k], len(v)) for k, v in self.prog.items()
                 if pos[k] < len(v)}
        if stuck:
            detail = {}
            for k in stuck:
                op = self.prog[k][pos[k]]
                detail[k] = op
            raise AssertionError(f"plan deadlock: {stuck} blocked at {detail} "
                                 f"sems={sems}")


class FakeIns:
    def then_inc(self, *a, **k):
        return self


class FakeEng:
    def __getattr__(self, name):
        return lambda *a, **k: FakeIns()

    def wait_ge(self, *a, **k):
        pass


def build_nc():
    nc = bass.Bass("TRN2", target_bir_lowering=False)

    xT = nc.dram_tensor("xT", [D, NTOK], F32, kind="ExternalInput")
    wkN = nc.dram_tensor("wkN", [D, D], F32, kind="ExternalInput")
    w1T = nc.dram_tensor("w1T", [D, DH], F32, kind="ExternalInput")
    w2T = nc.dram_tensor("w2T", [DH, D], F32, kind="ExternalInput")
    wgT = nc.dram_tensor("wgT", [D, H], I32, kind="ExternalInput")
    wuT = nc.dram_tensor("wuT", [D, H], I32, kind="ExternalInput")
    wdT = nc.dram_tensor("wdT", [H, D], I32, kind="ExternalInput")
    idk = nc.dram_tensor("idk", [128, 128], F32, kind="ExternalInput")
    outT = nc.dram_tensor("outT", [D, NTOK], F32, kind="ExternalOutput")

    xTr = xT.rearrange("(a p) t -> p a t", p=128)
    outTr = outT.rearrange("(a p) t -> p a t", p=128)

    es = contextlib.ExitStack()
    def sb(name, shape, dtype):
        return es.enter_context(nc.sbuf_tensor(name, shape, dtype))
    def psum(name):
        return es.enter_context(nc.psum_tensor(name, [128, 512], F32))
    def sem(name):
        return es.enter_context(nc.semaphore(name))

    with es:
        WK = sb("WK", [128, DT, D], BF16)
        W1 = sb("W1", [128, DT, DH], BF16)
        W2 = sb("W2", [DH, D], BF16)
        WG = sb("WG", [128, DT, H], BF16)
        WU = sb("WU", [128, DT, H], BF16)
        WD = sb("WD", [128, HT, D], BF16)
        IDK = sb("IDK", [128, 128], BF16)
        XF = sb("XF", [128, 3, DT, TC], BF16)
        OUTB = sb("OUTB", [128, 4, TC], F32)
        Y1 = sb("Y1", [128, 2, DT, TC], BF16)
        SQT = sb("SQT", [128, 4, TC], BF16)
        SG = sb("SG", [128, 4, TC], BF16)
        H2 = sb("H2", [128, 4, TC], BF16)
        AG = sb("AG", [DH, TC], BF16)
        ONES = sb("ONES", [128, 1], BF16)
        ONESROW = sb("ONESROW", [1, 128], BF16)
        EPST = sb("EPST", [128, 1], F32)
        SROW = sb("SROW", [1, 512], BF16)
        MUT = sb("MUT", [128, TC], F32)
        VART = sb("VART", [128, TC], F32)
        MSQT = sb("MSQT", [128, TC], F32)
        RT = sb("RT", [128, TC], F32)
        MUT2 = sb("MUT2", [128, TC], F32)
        VART2 = sb("VART2", [128, TC], F32)
        RT2 = sb("RT2", [128, TC], F32)
        MSQT2 = sb("MSQT2", [128, TC], F32)
        SUBT = sb("SUBT", [128, DT, TC], BF16)

        P = [psum(f"P{i}") for i in range(8)]
        PDB = [P[0], P[1], P[2], P[7]]
        PGU = [P[3], P[4]]
        def KB(g): return P[5] if g % 2 == 0 else P[6]
        def RB(g): return P[6] if g % 2 == 0 else P[5]
        PH = P[5]
        PST1, PBC1 = P[6], P[5]
        PST2, PBC2 = P[5], P[6]

        s_t = sem("s_t")
        s_v = sem("s_v")
        s_s = sem("s_s")
        s_x3 = [sem(f"s_x{k}") for k in range(3)]
        s_ob = [sem(f"s_ob{k}") for k in range(4)]
        s_wk = sem("s_wk")
        s_w1 = sem("s_w1")
        s_w2 = sem("s_w2")
        s_id = sem("s_id")
        s_g = sem("s_g")
        s_wg = [sem(f"s_wg{q}") for q in range(8)]
        s_wu = [sem(f"s_wu{q}") for q in range(8)]
        s_wd = [sem(f"s_wd{q}") for q in range(8)]

        E_ = Plan()

        def attn_item(t, item, c):
            if item == "H":
                E_.wait(t, s_x3[c % 3], 16 * (c // 3 + 1))
                E_.wait(t, s_v, ("v", -1, "W1K"))
                E_.wait(t, s_v, ("v", c - 2, "FIN2"))
                for a in range(DT):
                    ins = t.matmul(PH[0:DH, 0:TC], W1[:, a, :],
                                   XF[:, c % 3, a, :],
                                   start=(a == 0), stop=(a == DT - 1))
                E_.inc(ins, s_t, ("t", c, "H"))
            elif item.startswith("R"):
                g = int(item[1])
                bank = RB(g)
                if c == 0:
                    E_.wait(t, s_w2, 16)
                E_.wait(t, s_s, ("s", c, "AG"))
                if g == 0:
                    E_.wait(t, s_v, ("v", c - 2, "BCT2"))
                elif g == 1:
                    pass
                else:
                    E_.wait(t, s_v, ("v", c, f"SE{g - 2}"))
                for oh in range(2):
                    o = 2 * g + oh
                    ins = t.matmul(bank[:, oh * TC:(oh + 1) * TC],
                                   W2[0:DH, o * 128:(o + 1) * 128],
                                   AG[0:DH, :], start=(oh == 0), stop=(oh == 1))
                E_.inc(ins, s_t, ("t", c, f"R{g}"))
            elif item in ("ST1a", "ST1b"):
                lo = 0 if item == "ST1a" else 4
                if lo == 0:
                    E_.wait(t, s_v, ("v", c, "SE2"))
                for a in range(lo, lo + 4):
                    E_.wait(t, s_v, ("v", c, f"SE{a // 2}"))
                    t.matmul(PST1[0:1, 0:TC], ONES[:, 0:1],
                             XF[:, c % 3, a, :],
                             start=(a == 0), stop=False)
                    E_.wait(t, s_v, ("v", c, f"STQ{a}"))
                    E_.inc(t.matmul(PST1[0:1, TC:2 * TC], ONES[:, 0:1],
                                    SQT[:, a % 4, :],
                                    start=False, stop=(a == DT - 1)),
                           s_t, ("t", c, f"ST1_{a}"))
            elif item == "BC1":
                E_.wait(t, s_v, ("v", c, "SE3"))
                E_.wait(t, s_v, ("v", c, "FIN1"))
                E_.inc(t.matmul(PBC1[:, 0:512], ONESROW[0:1, 0:128],
                                SROW[0:1, 0:512], start=True, stop=True),
                       s_t, ("t", c, "BC1"))
            elif item in ("ST2a", "ST2b"):
                lo = 0 if item == "ST2a" else 4
                if lo == 0:
                    E_.wait(t, s_v, ("v", c + 1, "BCT1"))
                for a in range(lo, lo + 4):
                    E_.wait(t, s_s, ("s", c, f"DE{a // 2}"))
                    t.matmul(PST2[0:1, 0:TC], ONES[:, 0:1],
                             XF[:, c % 3, a, :],
                             start=(a == 0), stop=False)
                    E_.wait(t, s_v, ("v", c, f"SQ2_{a}"))
                    E_.inc(t.matmul(PST2[0:1, TC:2 * TC], ONES[:, 0:1],
                                    SQT[:, a % 4, :],
                                    start=False, stop=(a == DT - 1)),
                           s_t, ("t", c, f"ST2_{a}"))
            elif item == "BC2":
                E_.wait(t, s_v, ("v", c + 1, "FIN1"))
                E_.wait(t, s_v, ("v", c, "FIN2"))
                E_.inc(t.matmul(PBC2[:, 0:512], ONESROW[0:1, 0:128],
                                SROW[0:1, 0:512], start=True, stop=True),
                       s_t, ("t", c, "BC2"))
            else:
                raise AssertionError(item)

        def emit_tensor(t):
            E_.wait(t, s_wk, 16)
            E_.wait(t, s_w1, 16)
            for hb in range(DT):
                for kb in range(DT):
                    ins = t.matmul(P[5][:, hb * DH:(hb + 1) * DH],
                                   WK[:, kb, hb * 128:(hb + 1) * 128],
                                   W1[:, kb, :],
                                   start=(hb == 0 and kb == 0),
                                   stop=(hb == DT - 1 and kb == DT - 1))
            E_.inc(ins, s_t, ("t", -1, "W1K"))
            for item in ["H", "R0", "R1", "R2", "R3", "ST1a", "ST1b", "BC1"]:
                attn_item(t, item, 0)
            for i in range(NCH + 1):
                if i <= NCH - 1:
                    for kind, j in _FFN_SEQ:
                        if kind == "G":
                            bank = PGU[j % 2]
                            if i == 0:
                                E_.wait(t, s_wg[j // 3], 16)
                                E_.wait(t, s_wu[j // 3], 16)
                            if j >= 2:
                                E_.wait(t, s_v, ("v", i, f"H2_{j - 2}"))
                            else:
                                E_.wait(t, s_v, ("v", i, "Y1_3"))
                                E_.wait(t, s_v, ("v", i, "Y1_7"))
                                E_.wait(t, s_v, ("v", i - 1, f"H2_{22 + j}"))
                            for a in range(DT):
                                ins = t.matmul(bank[:, 0:TC],
                                               WG[:, a, j * 128:(j + 1) * 128],
                                               Y1[:, i % 2, a, :], start=(a == 0),
                                               stop=False)
                            for a in range(DT):
                                ins = t.matmul(bank[:, TC:2 * TC],
                                               WU[:, a, j * 128:(j + 1) * 128],
                                               Y1[:, i % 2, a, :], start=False,
                                               stop=(a == DT - 1))
                            E_.inc(ins, s_t, ("t", i, f"TG{j}"))
                        else:
                            if i == 0:
                                E_.wait(t, s_wd[j // 3], 16)
                                if j == 0:
                                    E_.wait(t, s_id, 16)
                            for item in PE_SLOTS.get(j, []):
                                if item in ("ST2a", "ST2b", "BC2"):
                                    if i >= 1:
                                        attn_item(t, item, i - 1)
                                elif i + 1 <= NCH - 1:
                                    attn_item(t, item, i + 1)
                            E_.wait(t, s_v, ("v", i, f"H2_{j}"))
                            if j == 0:
                                E_.wait(t, s_s, ("s", i - 1, "DE3"))
                                for o in range(DT):
                                    t.matmul(PDB[o // 2][:, (o % 2) * TC:(o % 2 + 1) * TC],
                                             IDK[:, :], Y1[:, i % 2, o, :],
                                             start=(o % 2 == 0), stop=False)
                            for o in range(DT):
                                ins = t.matmul(PDB[o // 2][:, (o % 2) * TC:(o % 2 + 1) * TC],
                                               WD[:, j, o * 128:(o + 1) * 128],
                                               H2[:, j % 4, :],
                                               start=False,
                                               stop=(j == HT - 1 and o % 2 == 1))
                            E_.inc(ins, s_t, ("t", i, f"TD{j}"))
                else:
                    for item in ("ST2a", "ST2b", "BC2"):
                        attn_item(t, item, i - 1)

        def dve_item(v, item, i):
            if item.startswith("SE"):
                c, g = i + 1, int(item[2])
                E_.wait(v, s_x3[c % 3], 16 * (c // 3 + 1))
                E_.wait(v, s_t, ("t", c, f"R{g}"))
                E_.inc(v.tensor_tensor(XF[:, c % 3, 2 * g:2 * g + 2, :],
                                       XF[:, c % 3, 2 * g:2 * g + 2, :],
                                       RB(g)[:, 0:512], OP.add),
                       s_v, ("v", c, f"SE{g}"))
            elif item == "FIN1":
                c = i + 1
                E_.wait(v, s_t, ("t", c, "ST1_7"))
                E_.wait(v, s_t, ("t", c - 2, "BC2"))
                E_.inc(v.tensor_copy(SROW[0:1, 0:512], PST1[0:1, 0:512]),
                       s_v, ("v", c, "FIN1"))
            elif item == "BCT1":
                c = i + 1
                E_.wait(v, s_t, ("t", c, "BC1"))
                E_.wait(v, s_v, ("v", c - 1, "Y1_7"))
                E_.chain(v.tensor_scalar(MUT[:], PBC1[:, 0:TC], INVD, None, OP.mult),
                         v, s_v)
                E_.chain(v.scalar_tensor_tensor(MSQT[:], MUT[:], -1.0, MUT[:],
                                                OP.mult, OP.mult), v, s_v)
                E_.inc(v.scalar_tensor_tensor(VART[:], PBC1[:, TC:2 * TC], INVD,
                                              MSQT[:], OP.mult, OP.add),
                       s_v, ("v", c, "BCT1"))
            elif item.startswith("Y1_"):
                c = i + 1
                E_.wait(v, s_v, ("v", c, "SE3"))
                E_.wait(v, s_s, ("s", c, "RS1"))
                E_.wait(v, s_t, ("t", c - 2, "TG23"))
                for a in (int(item[3]), int(item[4])):
                    E_.chain(v.tensor_tensor(XF[:, c % 3, a, :], XF[:, c % 3, a, :],
                                             MUT[:], OP.subtract), v, s_v)
                    E_.inc(v.tensor_tensor(Y1[:, c % 2, a, :], XF[:, c % 3, a, :],
                                           RT[:], OP.mult),
                           s_v, ("v", c, f"Y1_{a}"))
            elif item == "FIN2":
                c = i - 1
                E_.wait(v, s_t, ("t", c, "ST2_7"))
                E_.wait(v, s_t, ("t", c + 1, "BC1"))
                E_.inc(v.tensor_copy(SROW[0:1, 0:512], PST2[0:1, 0:512]),
                       s_v, ("v", c, "FIN2"))
            elif item == "BCT2":
                c = i - 1
                E_.wait(v, s_t, ("t", c, "BC2"))
                E_.wait(v, s_g, ("g", c - 1, "SUB7"))
                E_.chain(v.tensor_scalar(MUT2[:], PBC2[:, 0:TC], INVD, None, OP.mult),
                         v, s_v)
                E_.chain(v.scalar_tensor_tensor(MSQT[:], MUT2[:], -1.0, MUT2[:],
                                                OP.mult, OP.mult), v, s_v)
                E_.inc(v.scalar_tensor_tensor(VART2[:], PBC2[:, TC:2 * TC], INVD,
                                              MSQT[:], OP.mult, OP.add),
                       s_v, ("v", c, "BCT2"))
            elif item.startswith("STQ"):
                c = i + 1
                for a in (int(item[3]), int(item[4])):
                    E_.wait(v, s_v, ("v", c, f"SE{a // 2}"))
                    if a < 4:
                        E_.wait(v, s_t, ("t", c - 2, f"ST2_{4 + a}"))
                    else:
                        E_.wait(v, s_t, ("t", c, f"ST1_{a - 4}"))
                    E_.inc(v.tensor_tensor(SQT[:, a % 4, :],
                                           XF[:, c % 3, a, :],
                                           XF[:, c % 3, a, :], OP.mult),
                           s_v, ("v", c, f"STQ{a}"))
            elif item.startswith("Q2_"):
                c = i - 1
                for a in (int(item[3]), int(item[4])):
                    E_.wait(v, s_s, ("s", c, f"DE{a // 2}"))
                    if a < 4:
                        E_.wait(v, s_t, ("t", c + 1, f"ST1_{4 + a}"))
                    else:
                        E_.wait(v, s_t, ("t", c, f"ST2_{a - 4}"))
                    E_.inc(v.tensor_tensor(SQT[:, a % 4, :],
                                           XF[:, c % 3, a, :],
                                           XF[:, c % 3, a, :], OP.mult),
                           s_v, ("v", c, f"SQ2_{a}"))
            else:
                raise AssertionError(item)

        def emit_vector(v):
            v.memset(ONES[:], 1.0)
            v.memset(EPST[:], EPS)
            E_.inc(v.memset(ONESROW[:], 1.0), s_v, ("v", -1, "INIT"))
            E_.wait(v, s_t, ("t", -1, "W1K"))
            E_.inc(v.tensor_copy(W1[:, :, :], P[5][:, 0:512]),
                   s_v, ("v", -1, "W1K"))
            for item in ["SE0", "STQ01", "SE1", "STQ23", "SE2", "SE3",
                         "STQ45", "STQ67", "FIN1", "BCT1", "Y1_01", "Y1_23",
                         "Y1_45", "Y1_67"]:
                dve_item(v, item, -1)
            for i in range(NCH + 1):
                if i <= NCH - 1:
                    for j in range(HT):
                        E_.wait(v, s_s, ("s", i, f"SIG{j}"))
                        if j >= 4:
                            E_.wait(v, s_t, ("t", i, f"TD{j - 4}"))
                        else:
                            E_.wait(v, s_t, ("t", i - 1, f"TD{20 + j}"))
                        bank = PGU[j % 2]
                        E_.inc(v.tensor_tensor(H2[:, j % 4, :], SG[:, j % 4, :],
                                               bank[:, TC:2 * TC], OP.mult),
                               s_v, ("v", i, f"H2_{j}"))
                        for item in DVE_SLOTS.get(j, []):
                            if item in ("FIN2", "BCT2") or item.startswith("Q2_"):
                                if i >= 1:
                                    dve_item(v, item, i)
                            elif i + 1 <= NCH - 1:
                                dve_item(v, item, i)
                    dve_item(v, "Q2_01", i + 1)
                    dve_item(v, "Q2_23", i + 1)
                else:
                    for item in ["Q2_45", "Q2_67", "FIN2", "BCT2"]:
                        dve_item(v, item, i)
                    c = NCH - 1
                    E_.wait(v, s_v, ("v", c, "BCT2"))
                    E_.wait(v, s_g, ("g", c - 1, "OUT7"))
                    for a in range(DT):
                        E_.inc(v.tensor_tensor(SUBT[:, a, :], XF[:, c % 3, a, :],
                                               MUT2[:], OP.subtract),
                               s_v, ("v", c, f"SUB{a}"))
                    E_.wait(v, s_s, ("s", c, "RS2"))
                    E_.wait(v, s_v, ("v", c, "SUB7"))
                    for a in range(DT):
                        n = 8 * c + a
                        E_.wait(v, s_ob[n % 4], 16 * (n // 4))
                        E_.inc(v.tensor_tensor(OUTB[:, a % 4, :], SUBT[:, a, :],
                                               RT2[:], OP.mult),
                               s_v, ("v", c, f"OUT{a}"))

        def sc_item(sc, item, i):
            if item == "GELU":
                c = i + 1
                E_.wait(sc, s_t, ("t", c, "H"))
                E_.wait(sc, s_t, ("t", c - 1, "R3"))
                E_.inc(sc.activation(AG[0:DH, :], PH[0:DH, 0:TC], AF.Gelu),
                       s_s, ("s", c, "AG"))
            elif item == "RS1":
                c = i + 1
                E_.wait(sc, s_v, ("v", c, "BCT1"))
                E_.wait(sc, s_v, ("v", c - 1, "Y1_7"))
                E_.chain(sc.activation(MSQT2[:], VART[:], AF.Ln, bias=EPST[:, 0:1]),
                         sc, s_s)
                E_.inc(sc.activation(RT[:], MSQT2[:], AF.Exp, scale=-0.5),
                       s_s, ("s", c, "RS1"))
            elif item == "RS2":
                c = i - 1
                E_.wait(sc, s_v, ("v", c, "BCT2"))
                E_.wait(sc, s_g, ("g", c - 1, "OUT7"))
                E_.chain(sc.activation(MSQT2[:], VART2[:], AF.Ln, bias=EPST[:, 0:1]),
                         sc, s_s)
                E_.inc(sc.activation(RT2[:], MSQT2[:], AF.Exp, scale=-0.5),
                       s_s, ("s", c, "RS2"))
            elif item.startswith("DE"):
                c, g = i, int(item[2])
                E_.wait(sc, s_t, ("t", c, "TD23"))
                E_.inc(sc.activation(XF[:, c % 3, 2 * g:2 * g + 2, :],
                                     PDB[g][:, 0:512], AF.Copy, scale=KAPPA),
                       s_s, ("s", c, f"DE{g}"))
            else:
                raise AssertionError(item)

        def emit_scalar(sc):
            for item in ["GELU", "RS1"]:
                sc_item(sc, item, -1)
            for i in range(NCH + 1):
                if i <= NCH - 1:
                    for j in range(HT):
                        E_.wait(sc, s_t, ("t", i, f"TG{j}"))
                        if j >= 4:
                            E_.wait(sc, s_v, ("v", i, f"H2_{j - 4}"))
                        else:
                            E_.wait(sc, s_v, ("v", i - 1, f"H2_{20 + j}"))
                        E_.inc(sc.activation(SG[:, j % 4, :], PGU[j % 2][:, 0:TC],
                                             AF.Silu, scale=INV127),
                               s_s, ("s", i, f"SIG{j}"))
                        for item in SC_SLOTS.get(j, []):
                            if item == "RS2":
                                if i >= 1:
                                    sc_item(sc, item, i)
                            elif i + 1 <= NCH - 1:
                                sc_item(sc, item, i)
                    for g in range(4):
                        sc_item(sc, f"DE{g}", i)
                else:
                    sc_item(sc, "RS2", i)

        def gp_item(g, item, i):
            if item == "XL":
                c = i + 2
                E_.inc(g.dma_start(XF[:, c % 3], xTr[:, :, c * TC:(c + 1) * TC]),
                       s_x3[c % 3], None, 16)
            elif item.startswith("SUB"):
                c = i - 1
                E_.wait(g, s_v, ("v", c, "BCT2"))
                E_.mark(g, s_g)
                for a in (int(item[3]), int(item[4])):
                    E_.inc(g.tensor_tensor(SUBT[:, a, :], XF[:, c % 3, a, :],
                                           MUT2[:], OP.subtract),
                           s_g, ("g", c, f"SUB{a}"))
            elif item.startswith("MUL"):
                c = i - 1
                E_.wait(g, s_s, ("s", c, "RS2"))
                E_.mark(g, s_g)
                for a in (int(item[3]), int(item[4])):
                    n = 8 * c + a
                    if n >= 4:
                        E_.wait(g, s_ob[n % 4], 16 * (n // 4))
                    E_.inc(g.tensor_tensor(OUTB[:, a % 4, :], SUBT[:, a, :],
                                           RT2[:], OP.mult), s_g, ("g", c, f"OUT{a}"))
            else:
                raise AssertionError(item)

        def emit_gpsimd(g):
            wgr = wgT.rearrange("(a p) h -> p a h", p=128)
            wur = wuT.rearrange("(a p) h -> p a h", p=128)
            wdr = wdT.rearrange("(j p) o -> p j o", p=128)
            HQ = H // 8
            JQ = HT // 8
            E_.inc(g.dma_start(WK[:], wkN.rearrange("(kb p) h -> p kb h", p=128)),
                   s_wk, None, 16)
            E_.inc(g.dma_start(W1[:], w1T.rearrange("(a p) m -> p a m", p=128)),
                   s_w1, None, 16)
            E_.inc(g.dma_start(XF[:, 0], xTr[:, :, 0:TC]), s_x3[0], None, 16)
            E_.inc(g.dma_start(W2[:], w2T[:, :]), s_w2, None, 16)
            E_.inc(g.dma_start(IDK[:], idk[:, :]), s_id, None, 16)
            for q in range(8):
                E_.inc(g.dma_start(WG[:, :, q * HQ:(q + 1) * HQ],
                                   wgr[:, :, q * HQ:(q + 1) * HQ]), s_wg[q], None, 16)
                E_.inc(g.dma_start(WU[:, :, q * HQ:(q + 1) * HQ],
                                   wur[:, :, q * HQ:(q + 1) * HQ]), s_wu[q], None, 16)
                E_.inc(g.dma_start(WD[:, q * JQ:(q + 1) * JQ, :],
                                   wdr[:, q * JQ:(q + 1) * JQ, :]), s_wd[q], None, 16)
                if q == 0:
                    E_.inc(g.dma_start(XF[:, 1], xTr[:, :, TC:2 * TC]),
                           s_x3[1], None, 16)


            for i in range(NCH + 1):
                for item in GP_SEQ:
                    if item == "XL":
                        if i + 2 <= NCH - 1:
                            gp_item(g, item, i)
                    elif item.startswith("Y1_"):
                        if i + 1 <= NCH - 1:
                            gp_item(g, item, i)
                    else:
                        if 1 <= i <= NCH - 1:
                            gp_item(g, item, i)

        def emit_sync(s):
            for c in range(NCH):
                for a in range(DT):
                    n = 8 * c + a
                    if c == NCH - 1:
                        E_.wait(s, s_v, ("v", c, f"OUT{a}"))
                    else:
                        E_.wait(s, s_g, ("g", c, f"OUT{a}"))
                    E_.inc(s.dma_start(outTr[:, a, c * TC:(c + 1) * TC],
                                       OUTB[:, a % 4, :]), s_ob[n % 4], None, 16)
            if not E_.recording:
                for k in range(4):
                    s.wait_ge(s_ob[k], 16 * (8 * NCH // 4))

        fe = FakeEng()
        E_.begin_stream("g"); emit_gpsimd(fe)
        E_.begin_stream("q"); emit_sync(fe)
        E_.begin_stream("t"); emit_tensor(fe)
        E_.begin_stream("v"); emit_vector(fe)
        E_.begin_stream("s"); emit_scalar(fe)
        E_.verify_no_deadlock()
        E_.recording = False

        block = es.enter_context(nc.Block())

        @block.gpsimd
        def _(g):
            emit_gpsimd(g)

        @block.sync
        def _(s):
            emit_sync(s)

        @block.tensor
        def _(t):
            emit_tensor(t)

        @block.vector
        def _(v):
            emit_vector(v)

        @block.scalar
        def _(sc):
            emit_scalar(sc)

    return nc


_NC_CACHE = {}

def _get_nc():
    if "nc" not in _NC_CACHE:
        _NC_CACHE["nc"] = build_nc()
    return _NC_CACHE["nc"]


def kernel(L0_attn_x, L0_attn_W_K, L0_attn_W_V, L0_attn_W1, L0_attn_W2,
           L0_attn_m1, L0_attn_m2, L0_ffn_w_gate, L0_ffn_w_up, L0_ffn_w_down):
    x = np.ascontiguousarray(np.asarray(L0_attn_x, dtype=np.float32)).reshape(NCORES * NTOK, D)
    wkN = np.ascontiguousarray(np.asarray(L0_attn_W_K, dtype=np.float32))
    w1T = np.ascontiguousarray(np.asarray(L0_attn_W1, dtype=np.float32).T)
    w2T = np.ascontiguousarray(np.asarray(L0_attn_W2, dtype=np.float32).T)
    wgT = np.ascontiguousarray(np.asarray(L0_ffn_w_gate, dtype=np.int32).T)
    wuT = np.ascontiguousarray(np.asarray(L0_ffn_w_up, dtype=np.int32).T)
    wdT = np.ascontiguousarray(np.asarray(L0_ffn_w_down, dtype=np.int32).T)
    idk = np.eye(128, dtype=np.float32) * np.float32(INVKAPPA)

    nc = _get_nc()
    in_maps = []
    for i in range(NCORES):
        xTi = np.ascontiguousarray(x[i * NTOK:(i + 1) * NTOK, :].T)
        in_maps.append({"xT": xTi, "wkN": wkN, "w1T": w1T, "w2T": w2T,
                        "wgT": wgT, "wuT": wuT, "wdT": wdT, "idk": idk})
    res = run_bass_kernel_spmd(nc, in_maps, core_ids=list(range(NCORES)))
    outs = []
    for i in range(NCORES):
        outs.append(np.ascontiguousarray(np.asarray(res.results[i]["outT"]).T))
    return np.concatenate(outs, axis=0).reshape(NCORES, NTOK, D).astype(np.float32, copy=False)


# version 52
# speedup vs baseline: 1.1982x; 1.0095x over previous
import numpy as np
import contextlib
import concourse.bass as bass
import concourse.mybir as mybir
from concourse.bass_utils import run_bass_kernel_spmd
from concourse.alu_op_type import AluOpType

F32 = mybir.dt.float32
BF16 = mybir.dt.bfloat16
I32 = mybir.dt.int32
AF = mybir.ActivationFunctionType
OP = AluOpType

NCORES = 8
D = 1024
DT = 8
DH = 64
H = 3072
HT = 24
NTOK = 4096
TC = 256
NCH = NTOK // TC
KAPPA = float(1.0 / 127.0 ** 2)
INVKAPPA = float(127.0 ** 2)
INV127 = float(1.0 / 127.0)
INV_SQRT2 = 0.7071067811865476
EPS = 1e-5
INVD = float(1.0 / D)

_FFN_SEQ = []
for _j in range(HT):
    _FFN_SEQ.append(("G", _j))
    if _j >= 1:
        _FFN_SEQ.append(("D", _j - 1))
_FFN_SEQ.append(("D", HT - 1))

PE_SLOTS = {
    0: ["ST2a"], 1: ["ST2b"], 2: ["BC2"],
    5: ["H"], 7: ["R0"], 8: ["R1"], 9: ["R2"], 10: ["R3"],
    12: ["ST1a"], 13: ["ST1b"], 14: ["BC1"],
}
DVE_SLOTS = {
    2: ["FIN2"], 3: ["BCT2"],
    8: ["SE0"], 9: ["SE1", "STQ01"], 10: ["SE2", "STQ23"],
    11: ["SE3", "STQ45"], 12: ["STQ67", "FIN1"],
    14: ["BCT1"], 16: ["Y1_01"], 17: ["Y1_23"], 18: ["Y1_45", "Y1_67"],
}
GP_SEQ = ["Q2_01", "Q2_23", "Q2_45", "Q2_67",
          "SUB01", "SUB23", "SUB45", "SUB67",
          "MUL01", "MUL23", "MUL45", "MUL67", "XL"]
SC_SLOTS = {
    5: ["GELU"],
    7: ["RS2"],
    15: ["RS1"],
}


class WaitTracker:
    def __init__(self):
        self.seen = {}

    def need(self, sem, val):
        if val is None or val <= 0:
            return False
        if self.seen.get(sem.name, -1) >= val:
            return False
        self.seen[sem.name] = val
        return True


class Plan:
    def __init__(self):
        self.ev = {}
        self.cnt = {}
        self.recording = True
        self.trackers = {}
        self.prog = {}
        self.cur = None
        self.chain_vals = []
        self.chain_idx = 0

    def begin_stream(self, name):
        self.cur = self.prog.setdefault(name, [])

    def inc(self, ins, sem, key=None, amount=1):
        if self.recording:
            c = self.cnt.get(sem.name, 0) + amount
            self.cnt[sem.name] = c
            if key is not None:
                assert key not in self.ev, key
                self.ev[key] = c
            self.cur.append(("inc", sem.name, amount, key))
        else:
            ins.then_inc(sem, amount)

    def chain(self, ins, eng, sem):
        if self.recording:
            c = self.cnt.get(sem.name, 0) + 1
            self.cnt[sem.name] = c
            self.chain_vals.append(c)
            self.cur.append(("inc", sem.name, 1, None))
        else:
            ins.then_inc(sem, 1)
            eng.wait_ge(sem, self.chain_vals[self.chain_idx])
            self.chain_idx += 1

    def mark(self, eng, sem):
        if self.recording:
            self.chain_vals.append(self.cnt.get(sem.name, 0))
        else:
            val = self.chain_vals[self.chain_idx]
            self.chain_idx += 1
            if val > 0:
                eng.wait_ge(sem, val)

    def wait(self, eng, sem, key):
        if self.recording:
            if isinstance(key, int):
                if key > 0:
                    self.cur.append(("wait", sem.name, key, None))
            else:
                self.cur.append(("wait", sem.name, None, key))
            return
        if isinstance(key, int):
            val = key
        else:
            val = self.ev.get(key)
            if val is None:
                assert not (0 <= key[1] <= NCH - 1), f"missing event {key}"
                return
        tr = self.trackers.setdefault(id(eng), WaitTracker())
        if tr.need(sem, val):
            eng.wait_ge(sem, val)

    def verify_no_deadlock(self):
        pos = {k: 0 for k in self.prog}
        sems = {}
        progress = True
        while progress:
            progress = False
            for name, ops in self.prog.items():
                i = pos[name]
                while i < len(ops):
                    op, semn, val, key = ops[i]
                    if op == "wait":
                        if key is not None:
                            val = self.ev.get(key)
                        if val is not None and sems.get(semn, 0) < val:
                            break
                    else:
                        sems[semn] = sems.get(semn, 0) + val
                        progress = progress or True
                    i += 1
                    progress = True
                pos[name] = i
        stuck = {k: (pos[k], len(v)) for k, v in self.prog.items()
                 if pos[k] < len(v)}
        if stuck:
            detail = {}
            for k in stuck:
                op = self.prog[k][pos[k]]
                detail[k] = op
            raise AssertionError(f"plan deadlock: {stuck} blocked at {detail} "
                                 f"sems={sems}")


class FakeIns:
    def then_inc(self, *a, **k):
        return self


class FakeEng:
    def __getattr__(self, name):
        return lambda *a, **k: FakeIns()

    def wait_ge(self, *a, **k):
        pass


def build_nc():
    nc = bass.Bass("TRN2", target_bir_lowering=False)

    xT = nc.dram_tensor("xT", [D, NTOK], F32, kind="ExternalInput")
    wkN = nc.dram_tensor("wkN", [D, D], F32, kind="ExternalInput")
    w1T = nc.dram_tensor("w1T", [D, DH], F32, kind="ExternalInput")
    w2T = nc.dram_tensor("w2T", [DH, D], F32, kind="ExternalInput")
    wgT = nc.dram_tensor("wgT", [D, H], I32, kind="ExternalInput")
    wuT = nc.dram_tensor("wuT", [D, H], I32, kind="ExternalInput")
    wdT = nc.dram_tensor("wdT", [H, D], I32, kind="ExternalInput")
    idk = nc.dram_tensor("idk", [128, 128], F32, kind="ExternalInput")
    outT = nc.dram_tensor("outT", [D, NTOK], F32, kind="ExternalOutput")

    xTr = xT.rearrange("(a p) t -> p a t", p=128)
    outTr = outT.rearrange("(a p) t -> p a t", p=128)

    es = contextlib.ExitStack()
    def sb(name, shape, dtype):
        return es.enter_context(nc.sbuf_tensor(name, shape, dtype))
    def psum(name):
        return es.enter_context(nc.psum_tensor(name, [128, 512], F32))
    def sem(name):
        return es.enter_context(nc.semaphore(name))

    with es:
        WK = sb("WK", [128, DT, D], BF16)
        W1 = sb("W1", [128, DT, DH], BF16)
        W2 = sb("W2", [DH, D], BF16)
        WG = sb("WG", [128, DT, H], BF16)
        WU = sb("WU", [128, DT, H], BF16)
        WD = sb("WD", [128, HT, D], BF16)
        IDK = sb("IDK", [128, 128], BF16)
        XF = sb("XF", [128, 3, DT, TC], BF16)
        OUTB = sb("OUTB", [128, 4, TC], F32)
        Y1 = sb("Y1", [128, 2, DT, TC], BF16)
        SQT = sb("SQT", [128, 4, TC], BF16)
        SG = sb("SG", [128, 4, TC], BF16)
        H2 = sb("H2", [128, 4, TC], BF16)
        AG = sb("AG", [DH, TC], BF16)
        ONES = sb("ONES", [128, 1], BF16)
        ONESROW = sb("ONESROW", [1, 128], BF16)
        EPST = sb("EPST", [128, 1], F32)
        SROW = sb("SROW", [1, 512], BF16)
        MUT = sb("MUT", [128, TC], F32)
        VART = sb("VART", [128, TC], F32)
        MSQT = sb("MSQT", [128, TC], F32)
        RT = sb("RT", [128, TC], F32)
        MUT2 = sb("MUT2", [128, TC], F32)
        VART2 = sb("VART2", [128, TC], F32)
        RT2 = sb("RT2", [128, TC], F32)
        MSQT2 = sb("MSQT2", [128, TC], F32)
        SUBT = sb("SUBT", [128, DT, TC], BF16)

        P = [psum(f"P{i}") for i in range(8)]
        PDB = [P[0], P[1], P[2], P[7]]
        PGU = [P[3], P[4]]
        def KB(g): return P[5] if g % 2 == 0 else P[6]
        def RB(g): return P[6] if g % 2 == 0 else P[5]
        PH = P[5]
        PST1, PBC1 = P[6], P[5]
        PST2, PBC2 = P[5], P[6]

        s_t = sem("s_t")
        s_v = sem("s_v")
        s_s = sem("s_s")
        s_x3 = [sem(f"s_x{k}") for k in range(3)]
        s_ob = [sem(f"s_ob{k}") for k in range(4)]
        s_wk = sem("s_wk")
        s_w1 = sem("s_w1")
        s_w2 = sem("s_w2")
        s_id = sem("s_id")
        s_g = sem("s_g")
        s_wg = [sem(f"s_wg{q}") for q in range(8)]
        s_wu = [sem(f"s_wu{q}") for q in range(8)]
        s_wd = [sem(f"s_wd{q}") for q in range(8)]

        E_ = Plan()

        def attn_item(t, item, c):
            if item == "H":
                E_.wait(t, s_x3[c % 3], 16 * (c // 3 + 1))
                E_.wait(t, s_v, ("v", -1, "W1K"))
                E_.wait(t, s_v, ("v", c - 2, "FIN2"))
                for a in range(DT):
                    ins = t.matmul(PH[0:DH, 0:TC], W1[:, a, :],
                                   XF[:, c % 3, a, :],
                                   start=(a == 0), stop=(a == DT - 1))
                E_.inc(ins, s_t, ("t", c, "H"))
            elif item.startswith("R"):
                g = int(item[1])
                bank = RB(g)
                if c == 0:
                    E_.wait(t, s_w2, 16)
                E_.wait(t, s_s, ("s", c, "AG"))
                if g == 0:
                    E_.wait(t, s_v, ("v", c - 2, "BCT2"))
                elif g == 1:
                    pass
                else:
                    E_.wait(t, s_v, ("v", c, f"SE{g - 2}"))
                for oh in range(2):
                    o = 2 * g + oh
                    ins = t.matmul(bank[:, oh * TC:(oh + 1) * TC],
                                   W2[0:DH, o * 128:(o + 1) * 128],
                                   AG[0:DH, :], start=(oh == 0), stop=(oh == 1))
                E_.inc(ins, s_t, ("t", c, f"R{g}"))
            elif item in ("ST1a", "ST1b"):
                lo = 0 if item == "ST1a" else 4
                if lo == 0:
                    E_.wait(t, s_v, ("v", c, "SE2"))
                for a in range(lo, lo + 4):
                    E_.wait(t, s_v, ("v", c, f"SE{a // 2}"))
                    t.matmul(PST1[0:1, 0:TC], ONES[:, 0:1],
                             XF[:, c % 3, a, :],
                             start=(a == 0), stop=False)
                    E_.wait(t, s_v, ("v", c, f"STQ{a}"))
                    E_.inc(t.matmul(PST1[0:1, TC:2 * TC], ONES[:, 0:1],
                                    SQT[:, a % 4, :],
                                    start=False, stop=(a == DT - 1)),
                           s_t, ("t", c, f"ST1_{a}"))
            elif item == "BC1":
                E_.wait(t, s_v, ("v", c, "SE3"))
                E_.wait(t, s_v, ("v", c, "FIN1"))
                E_.inc(t.matmul(PBC1[:, 0:512], ONESROW[0:1, 0:128],
                                SROW[0:1, 0:512], start=True, stop=True),
                       s_t, ("t", c, "BC1"))
            elif item in ("ST2a", "ST2b"):
                lo = 0 if item == "ST2a" else 4
                if lo == 0:
                    E_.wait(t, s_v, ("v", c + 1, "BCT1"))
                for a in range(lo, lo + 4):
                    E_.wait(t, s_s, ("s", c, f"DE{a // 2}"))
                    t.matmul(PST2[0:1, 0:TC], ONES[:, 0:1],
                             XF[:, c % 3, a, :],
                             start=(a == 0), stop=False)
                    E_.wait(t, s_g, ("g", c, f"SQ2_{a}"))
                    E_.inc(t.matmul(PST2[0:1, TC:2 * TC], ONES[:, 0:1],
                                    SQT[:, a % 4, :],
                                    start=False, stop=(a == DT - 1)),
                           s_t, ("t", c, f"ST2_{a}"))
            elif item == "BC2":
                E_.wait(t, s_v, ("v", c + 1, "FIN1"))
                E_.wait(t, s_v, ("v", c, "FIN2"))
                E_.inc(t.matmul(PBC2[:, 0:512], ONESROW[0:1, 0:128],
                                SROW[0:1, 0:512], start=True, stop=True),
                       s_t, ("t", c, "BC2"))
            else:
                raise AssertionError(item)

        def emit_tensor(t):
            E_.wait(t, s_wk, 16)
            E_.wait(t, s_w1, 16)
            for hb in range(DT):
                for kb in range(DT):
                    ins = t.matmul(P[5][:, hb * DH:(hb + 1) * DH],
                                   WK[:, kb, hb * 128:(hb + 1) * 128],
                                   W1[:, kb, :],
                                   start=(hb == 0 and kb == 0),
                                   stop=(hb == DT - 1 and kb == DT - 1))
            E_.inc(ins, s_t, ("t", -1, "W1K"))
            for item in ["H", "R0", "R1", "R2", "R3", "ST1a", "ST1b", "BC1"]:
                attn_item(t, item, 0)
            for i in range(NCH + 1):
                if i <= NCH - 1:
                    for kind, j in _FFN_SEQ:
                        if kind == "G":
                            bank = PGU[j % 2]
                            if i == 0:
                                E_.wait(t, s_wg[j // 3], 16)
                                E_.wait(t, s_wu[j // 3], 16)
                            if j >= 2:
                                E_.wait(t, s_v, ("v", i, f"H2_{j - 2}"))
                            else:
                                E_.wait(t, s_v, ("v", i, "Y1_3"))
                                E_.wait(t, s_v, ("v", i, "Y1_7"))
                                E_.wait(t, s_v, ("v", i - 1, f"H2_{22 + j}"))
                            for a in range(DT):
                                ins = t.matmul(bank[:, 0:TC],
                                               WG[:, a, j * 128:(j + 1) * 128],
                                               Y1[:, i % 2, a, :], start=(a == 0),
                                               stop=False)
                            for a in range(DT):
                                ins = t.matmul(bank[:, TC:2 * TC],
                                               WU[:, a, j * 128:(j + 1) * 128],
                                               Y1[:, i % 2, a, :], start=False,
                                               stop=(a == DT - 1))
                            E_.inc(ins, s_t, ("t", i, f"TG{j}"))
                        else:
                            if i == 0:
                                E_.wait(t, s_wd[j // 3], 16)
                                if j == 0:
                                    E_.wait(t, s_id, 16)
                            for item in PE_SLOTS.get(j, []):
                                if item in ("ST2a", "ST2b", "BC2"):
                                    if i >= 1:
                                        attn_item(t, item, i - 1)
                                elif i + 1 <= NCH - 1:
                                    attn_item(t, item, i + 1)
                            E_.wait(t, s_v, ("v", i, f"H2_{j}"))
                            if j == 0:
                                E_.wait(t, s_s, ("s", i - 1, "DE3"))
                                for o in range(DT):
                                    t.matmul(PDB[o // 2][:, (o % 2) * TC:(o % 2 + 1) * TC],
                                             IDK[:, :], Y1[:, i % 2, o, :],
                                             start=(o % 2 == 0), stop=False)
                            for o in range(DT):
                                ins = t.matmul(PDB[o // 2][:, (o % 2) * TC:(o % 2 + 1) * TC],
                                               WD[:, j, o * 128:(o + 1) * 128],
                                               H2[:, j % 4, :],
                                               start=False,
                                               stop=(j == HT - 1 and o % 2 == 1))
                            E_.inc(ins, s_t, ("t", i, f"TD{j}"))
                else:
                    for item in ("ST2a", "ST2b", "BC2"):
                        attn_item(t, item, i - 1)

        def dve_item(v, item, i):
            if item.startswith("SE"):
                c, g = i + 1, int(item[2])
                E_.wait(v, s_x3[c % 3], 16 * (c // 3 + 1))
                E_.wait(v, s_t, ("t", c, f"R{g}"))
                E_.inc(v.tensor_tensor(XF[:, c % 3, 2 * g:2 * g + 2, :],
                                       XF[:, c % 3, 2 * g:2 * g + 2, :],
                                       RB(g)[:, 0:512], OP.add),
                       s_v, ("v", c, f"SE{g}"))
            elif item == "FIN1":
                c = i + 1
                E_.wait(v, s_t, ("t", c, "ST1_7"))
                E_.wait(v, s_t, ("t", c - 2, "BC2"))
                E_.inc(v.tensor_copy(SROW[0:1, 0:512], PST1[0:1, 0:512]),
                       s_v, ("v", c, "FIN1"))
            elif item == "BCT1":
                c = i + 1
                E_.wait(v, s_t, ("t", c, "BC1"))
                E_.wait(v, s_v, ("v", c - 1, "Y1_7"))
                E_.chain(v.tensor_scalar(MUT[:], PBC1[:, 0:TC], INVD, None, OP.mult),
                         v, s_v)
                E_.chain(v.scalar_tensor_tensor(MSQT[:], MUT[:], -1.0, MUT[:],
                                                OP.mult, OP.mult), v, s_v)
                E_.inc(v.scalar_tensor_tensor(VART[:], PBC1[:, TC:2 * TC], INVD,
                                              MSQT[:], OP.mult, OP.add),
                       s_v, ("v", c, "BCT1"))
            elif item.startswith("Y1_"):
                c = i + 1
                E_.wait(v, s_v, ("v", c, "SE3"))
                E_.wait(v, s_s, ("s", c, "RS1"))
                E_.wait(v, s_t, ("t", c - 2, "TG23"))
                for a in (int(item[3]), int(item[4])):
                    E_.chain(v.tensor_tensor(XF[:, c % 3, a, :], XF[:, c % 3, a, :],
                                             MUT[:], OP.subtract), v, s_v)
                    E_.inc(v.tensor_tensor(Y1[:, c % 2, a, :], XF[:, c % 3, a, :],
                                           RT[:], OP.mult),
                           s_v, ("v", c, f"Y1_{a}"))
            elif item == "FIN2":
                c = i - 1
                E_.wait(v, s_t, ("t", c, "ST2_7"))
                E_.wait(v, s_t, ("t", c + 1, "BC1"))
                E_.inc(v.tensor_copy(SROW[0:1, 0:512], PST2[0:1, 0:512]),
                       s_v, ("v", c, "FIN2"))
            elif item == "BCT2":
                c = i - 1
                E_.wait(v, s_t, ("t", c, "BC2"))
                E_.wait(v, s_g, ("g", c - 1, "SUB7"))
                E_.chain(v.tensor_scalar(MUT2[:], PBC2[:, 0:TC], INVD, None, OP.mult),
                         v, s_v)
                E_.chain(v.scalar_tensor_tensor(MSQT[:], MUT2[:], -1.0, MUT2[:],
                                                OP.mult, OP.mult), v, s_v)
                E_.inc(v.scalar_tensor_tensor(VART2[:], PBC2[:, TC:2 * TC], INVD,
                                              MSQT[:], OP.mult, OP.add),
                       s_v, ("v", c, "BCT2"))
            elif item.startswith("STQ"):
                c = i + 1
                for a in (int(item[3]), int(item[4])):
                    E_.wait(v, s_v, ("v", c, f"SE{a // 2}"))
                    if a < 4:
                        E_.wait(v, s_t, ("t", c - 2, f"ST2_{4 + a}"))
                    else:
                        E_.wait(v, s_t, ("t", c, f"ST1_{a - 4}"))
                    E_.inc(v.tensor_tensor(SQT[:, a % 4, :],
                                           XF[:, c % 3, a, :],
                                           XF[:, c % 3, a, :], OP.mult),
                           s_v, ("v", c, f"STQ{a}"))
            else:
                raise AssertionError(item)

        def emit_vector(v):
            v.memset(ONES[:], 1.0)
            v.memset(EPST[:], EPS)
            E_.inc(v.memset(ONESROW[:], 1.0), s_v, ("v", -1, "INIT"))
            E_.wait(v, s_t, ("t", -1, "W1K"))
            E_.inc(v.tensor_copy(W1[:, :, :], P[5][:, 0:512]),
                   s_v, ("v", -1, "W1K"))
            for item in ["SE0", "STQ01", "SE1", "STQ23", "SE2", "SE3",
                         "STQ45", "STQ67", "FIN1", "BCT1", "Y1_01", "Y1_23",
                         "Y1_45", "Y1_67"]:
                dve_item(v, item, -1)
            for i in range(NCH + 1):
                if i <= NCH - 1:
                    for j in range(HT):
                        E_.wait(v, s_s, ("s", i, f"SIG{j}"))
                        if j >= 4:
                            E_.wait(v, s_t, ("t", i, f"TD{j - 4}"))
                        else:
                            E_.wait(v, s_t, ("t", i - 1, f"TD{20 + j}"))
                        bank = PGU[j % 2]
                        E_.inc(v.tensor_tensor(H2[:, j % 4, :], SG[:, j % 4, :],
                                               bank[:, TC:2 * TC], OP.mult),
                               s_v, ("v", i, f"H2_{j}"))
                        for item in DVE_SLOTS.get(j, []):
                            if item in ("FIN2", "BCT2") or item.startswith("Q2_"):
                                if i >= 1:
                                    dve_item(v, item, i)
                            elif i + 1 <= NCH - 1:
                                dve_item(v, item, i)
                else:
                    for item in ["FIN2", "BCT2"]:
                        dve_item(v, item, i)
                    c = NCH - 1
                    E_.wait(v, s_v, ("v", c, "BCT2"))
                    E_.wait(v, s_g, ("g", c - 1, "OUT7"))
                    for a in range(DT):
                        E_.inc(v.tensor_tensor(SUBT[:, a, :], XF[:, c % 3, a, :],
                                               MUT2[:], OP.subtract),
                               s_v, ("v", c, f"SUB{a}"))
                    E_.wait(v, s_s, ("s", c, "RS2"))
                    E_.wait(v, s_v, ("v", c, "SUB7"))
                    for a in range(DT):
                        n = 8 * c + a
                        E_.wait(v, s_ob[n % 4], 16 * (n // 4))
                        E_.inc(v.tensor_tensor(OUTB[:, a % 4, :], SUBT[:, a, :],
                                               RT2[:], OP.mult),
                               s_v, ("v", c, f"OUT{a}"))

        def sc_item(sc, item, i):
            if item == "GELU":
                c = i + 1
                E_.wait(sc, s_t, ("t", c, "H"))
                E_.wait(sc, s_t, ("t", c - 1, "R3"))
                E_.inc(sc.activation(AG[0:DH, :], PH[0:DH, 0:TC], AF.Gelu),
                       s_s, ("s", c, "AG"))
            elif item == "RS1":
                c = i + 1
                E_.wait(sc, s_v, ("v", c, "BCT1"))
                E_.wait(sc, s_v, ("v", c - 1, "Y1_7"))
                E_.chain(sc.activation(MSQT2[:], VART[:], AF.Ln, bias=EPST[:, 0:1]),
                         sc, s_s)
                E_.inc(sc.activation(RT[:], MSQT2[:], AF.Exp, scale=-0.5),
                       s_s, ("s", c, "RS1"))
            elif item == "RS2":
                c = i - 1
                E_.wait(sc, s_v, ("v", c, "BCT2"))
                E_.wait(sc, s_g, ("g", c - 1, "OUT7"))
                E_.chain(sc.activation(MSQT2[:], VART2[:], AF.Ln, bias=EPST[:, 0:1]),
                         sc, s_s)
                E_.inc(sc.activation(RT2[:], MSQT2[:], AF.Exp, scale=-0.5),
                       s_s, ("s", c, "RS2"))
            elif item.startswith("DE"):
                c, g = i, int(item[2])
                E_.wait(sc, s_t, ("t", c, "TD23"))
                E_.inc(sc.activation(XF[:, c % 3, 2 * g:2 * g + 2, :],
                                     PDB[g][:, 0:512], AF.Copy, scale=KAPPA),
                       s_s, ("s", c, f"DE{g}"))
            else:
                raise AssertionError(item)

        def emit_scalar(sc):
            for item in ["GELU", "RS1"]:
                sc_item(sc, item, -1)
            for i in range(NCH + 1):
                if i <= NCH - 1:
                    for j in range(HT):
                        E_.wait(sc, s_t, ("t", i, f"TG{j}"))
                        if j >= 4:
                            E_.wait(sc, s_v, ("v", i, f"H2_{j - 4}"))
                        else:
                            E_.wait(sc, s_v, ("v", i - 1, f"H2_{20 + j}"))
                        E_.inc(sc.activation(SG[:, j % 4, :], PGU[j % 2][:, 0:TC],
                                             AF.Silu, scale=INV127),
                               s_s, ("s", i, f"SIG{j}"))
                        for item in SC_SLOTS.get(j, []):
                            if item == "RS2":
                                if i >= 1:
                                    sc_item(sc, item, i)
                            elif i + 1 <= NCH - 1:
                                sc_item(sc, item, i)
                    for g in range(4):
                        sc_item(sc, f"DE{g}", i)
                else:
                    sc_item(sc, "RS2", i)

        def gp_item(g, item, i):
            if item == "XL":
                c = i + 2
                E_.inc(g.dma_start(XF[:, c % 3], xTr[:, :, c * TC:(c + 1) * TC]),
                       s_x3[c % 3], None, 16)
            elif item.startswith("Q2_"):
                c = i - 1
                for a in (int(item[3]), int(item[4])):
                    E_.wait(g, s_s, ("s", c, f"DE{a // 2}"))
                    if a < 4:
                        E_.wait(g, s_t, ("t", c + 1, f"ST1_{4 + a}"))
                    else:
                        E_.wait(g, s_t, ("t", c, f"ST2_{a - 4}"))
                    E_.inc(g.tensor_tensor(SQT[:, a % 4, :],
                                           XF[:, c % 3, a, :],
                                           XF[:, c % 3, a, :], OP.mult),
                           s_g, ("g", c, f"SQ2_{a}"))
            elif item.startswith("SUB"):
                c = i - 1
                E_.wait(g, s_v, ("v", c, "BCT2"))
                E_.mark(g, s_g)
                for a in (int(item[3]), int(item[4])):
                    E_.inc(g.tensor_tensor(SUBT[:, a, :], XF[:, c % 3, a, :],
                                           MUT2[:], OP.subtract),
                           s_g, ("g", c, f"SUB{a}"))
            elif item.startswith("MUL"):
                c = i - 1
                E_.wait(g, s_s, ("s", c, "RS2"))
                E_.mark(g, s_g)
                for a in (int(item[3]), int(item[4])):
                    n = 8 * c + a
                    if n >= 4:
                        E_.wait(g, s_ob[n % 4], 16 * (n // 4))
                    E_.inc(g.tensor_tensor(OUTB[:, a % 4, :], SUBT[:, a, :],
                                           RT2[:], OP.mult), s_g, ("g", c, f"OUT{a}"))
            else:
                raise AssertionError(item)

        def emit_gpsimd(g):
            wgr = wgT.rearrange("(a p) h -> p a h", p=128)
            wur = wuT.rearrange("(a p) h -> p a h", p=128)
            wdr = wdT.rearrange("(j p) o -> p j o", p=128)
            HQ = H // 8
            JQ = HT // 8
            E_.inc(g.dma_start(WK[:], wkN.rearrange("(kb p) h -> p kb h", p=128)),
                   s_wk, None, 16)
            E_.inc(g.dma_start(W1[:], w1T.rearrange("(a p) m -> p a m", p=128)),
                   s_w1, None, 16)
            E_.inc(g.dma_start(XF[:, 0], xTr[:, :, 0:TC]), s_x3[0], None, 16)
            E_.inc(g.dma_start(W2[:], w2T[:, :]), s_w2, None, 16)
            E_.inc(g.dma_start(IDK[:], idk[:, :]), s_id, None, 16)
            for q in range(8):
                E_.inc(g.dma_start(WG[:, :, q * HQ:(q + 1) * HQ],
                                   wgr[:, :, q * HQ:(q + 1) * HQ]), s_wg[q], None, 16)
                E_.inc(g.dma_start(WU[:, :, q * HQ:(q + 1) * HQ],
                                   wur[:, :, q * HQ:(q + 1) * HQ]), s_wu[q], None, 16)
                E_.inc(g.dma_start(WD[:, q * JQ:(q + 1) * JQ, :],
                                   wdr[:, q * JQ:(q + 1) * JQ, :]), s_wd[q], None, 16)
                if q == 0:
                    E_.inc(g.dma_start(XF[:, 1], xTr[:, :, TC:2 * TC]),
                           s_x3[1], None, 16)


            for i in range(NCH + 1):
                for item in GP_SEQ:
                    if item == "XL":
                        if i + 2 <= NCH - 1:
                            gp_item(g, item, i)
                    elif item.startswith("Q2_"):
                        if i >= 1:
                            gp_item(g, item, i)
                    else:
                        if 1 <= i <= NCH - 1:
                            gp_item(g, item, i)

        def emit_sync(s):
            for c in range(NCH):
                for a in range(DT):
                    n = 8 * c + a
                    if c == NCH - 1:
                        E_.wait(s, s_v, ("v", c, f"OUT{a}"))
                    else:
                        E_.wait(s, s_g, ("g", c, f"OUT{a}"))
                    E_.inc(s.dma_start(outTr[:, a, c * TC:(c + 1) * TC],
                                       OUTB[:, a % 4, :]), s_ob[n % 4], None, 16)
            if not E_.recording:
                for k in range(4):
                    s.wait_ge(s_ob[k], 16 * (8 * NCH // 4))

        fe = FakeEng()
        E_.begin_stream("g"); emit_gpsimd(fe)
        E_.begin_stream("q"); emit_sync(fe)
        E_.begin_stream("t"); emit_tensor(fe)
        E_.begin_stream("v"); emit_vector(fe)
        E_.begin_stream("s"); emit_scalar(fe)
        E_.verify_no_deadlock()
        E_.recording = False

        block = es.enter_context(nc.Block())

        @block.gpsimd
        def _(g):
            emit_gpsimd(g)

        @block.sync
        def _(s):
            emit_sync(s)

        @block.tensor
        def _(t):
            emit_tensor(t)

        @block.vector
        def _(v):
            emit_vector(v)

        @block.scalar
        def _(sc):
            emit_scalar(sc)

    return nc


_NC_CACHE = {}

def _get_nc():
    if "nc" not in _NC_CACHE:
        _NC_CACHE["nc"] = build_nc()
    return _NC_CACHE["nc"]


def kernel(L0_attn_x, L0_attn_W_K, L0_attn_W_V, L0_attn_W1, L0_attn_W2,
           L0_attn_m1, L0_attn_m2, L0_ffn_w_gate, L0_ffn_w_up, L0_ffn_w_down):
    x = np.ascontiguousarray(np.asarray(L0_attn_x, dtype=np.float32)).reshape(NCORES * NTOK, D)
    wkN = np.ascontiguousarray(np.asarray(L0_attn_W_K, dtype=np.float32))
    w1T = np.ascontiguousarray(np.asarray(L0_attn_W1, dtype=np.float32).T)
    w2T = np.ascontiguousarray(np.asarray(L0_attn_W2, dtype=np.float32).T)
    wgT = np.ascontiguousarray(np.asarray(L0_ffn_w_gate, dtype=np.int32).T)
    wuT = np.ascontiguousarray(np.asarray(L0_ffn_w_up, dtype=np.int32).T)
    wdT = np.ascontiguousarray(np.asarray(L0_ffn_w_down, dtype=np.int32).T)
    idk = np.eye(128, dtype=np.float32) * np.float32(INVKAPPA)

    nc = _get_nc()
    in_maps = []
    for i in range(NCORES):
        xTi = np.ascontiguousarray(x[i * NTOK:(i + 1) * NTOK, :].T)
        in_maps.append({"xT": xTi, "wkN": wkN, "w1T": w1T, "w2T": w2T,
                        "wgT": wgT, "wuT": wuT, "wdT": wdT, "idk": idk})
    res = run_bass_kernel_spmd(nc, in_maps, core_ids=list(range(NCORES)))
    outs = []
    for i in range(NCORES):
        outs.append(np.ascontiguousarray(np.asarray(res.results[i]["outT"]).T))
    return np.concatenate(outs, axis=0).reshape(NCORES, NTOK, D).astype(np.float32, copy=False)
